# version 38
# baseline (speedup 1.0000x reference)
import numpy as np
import ml_dtypes
from contextlib import ExitStack

import concourse.bass as bass
import concourse.mybir as mybir
from concourse.bass_utils import run_bass_kernel_spmd

F32 = mybir.dt.float32
BF16 = mybir.dt.bfloat16
AF = mybir.ActivationFunctionType
ALU = mybir.AluOpType

PI = float(np.pi)
TWO_PI = float(2 * np.pi)
TWO_PI_LO = float(np.nextafter(np.float32(TWO_PI), np.float32(0.0)))

T_SEQ = 4096
D = 1024
H = 8
HD = 128
NCH = 8
C = 512
NG = T_SEQ // C
NT = T_SEQ // 128
LEV = 12
EPS_LN = 1e-5
EPS_SRMS = 1e-6

PC_QKVU_B = 0
PC_MU_R = 32
PC_MU_K = 40
PC_THETA = 48
PC_SLOPE = 56
PCOLS = 64
ROWS_N = 2 * D + 128

CONST_VALS = [-PI, -PI / 2, EPS_LN, EPS_SRMS, 1.0 / D, PI / 2, -1.0, 2.0, -TWO_PI, 0.5, 1.5 * PI]

DEBUG_TAPS = []
TRI_SCALAR_EVAC = False


def _dbg(n):
    return n in DEBUG_TAPS


class Prog:

    ENGS = ("sync", "tensor", "vector", "scalar", "gpsimd")

    def __init__(self, nc):
        self.nc = nc
        self.ops = {e: [] for e in self.ENGS}
        self.sems = {}
        self.counts = {}

    def sem(self, name):
        self.sems[name] = self.nc.alloc_semaphore(name)
        self.counts[name] = 0
        return name

    def wait(self, eng, sem, val=None):
        if val is None:
            val = self.counts[sem]
        assert val <= self.counts[sem], (eng, sem, val, self.counts[sem])
        if val <= 0:
            return
        h = self.sems[sem]
        self.ops[eng].append(lambda e, h=h, v=val: e.wait_ge(h, v))

    def emit(self, eng, fn, inc=None, inc_n=1):
        if inc is not None:
            h = self.sems[inc]
            self.ops[eng].append(lambda e, fn=fn, h=h, n=inc_n: fn(e).then_inc(h, n))
            self.counts[inc] += inc_n
            return self.counts[inc]
        self.ops[eng].append(fn)
        return None

    def dma(self, eng, out, in_, sem):
        return self.emit(eng, lambda e, o=out, i=in_: e.dma_start(out=o, in_=i),
                         inc=sem, inc_n=16)

    def replay(self, block):
        def mk(name):
            def run(eng):
                for fn in self.ops[name]:
                    fn(eng)
            return run
        block.sync(mk("sync"))
        block.tensor(mk("tensor"))
        block.vector(mk("vector"))
        block.scalar(mk("scalar"))
        block.gpsimd(mk("gpsimd"))


def build_program():
    nc = bass.Bass("TRN2", target_bir_lowering=False)

    x_in = nc.dram_tensor("x_in", [T_SEQ, D], F32, kind="ExternalInput").ap()
    wqkvuT = nc.dram_tensor("wqkvuT", [D, 4 * D], F32, kind="ExternalInput").ap()
    woT_in = nc.dram_tensor("woT", [D, D], F32, kind="ExternalInput").ap()
    wrT_in = nc.dram_tensor("wrT", [D, D], F32, kind="ExternalInput").ap()
    wkT_in = nc.dram_tensor("wkT", [D, D], F32, kind="ExternalInput").ap()
    wvT_in = nc.dram_tensor("wvT", [D, D], F32, kind="ExternalInput").ap()
    params = nc.dram_tensor("params", [128, PCOLS], F32, kind="ExternalInput").ap()
    rows_in = nc.dram_tensor("rows", [1, ROWS_N], F32, kind="ExternalInput").ap()
    consts_in = nc.dram_tensor("consts", [128, len(CONST_VALS)], F32, kind="ExternalInput").ap()
    cones_in = nc.dram_tensor("cones", [128, 2], F32, kind="ExternalInput").ap()
    ident_in = nc.dram_tensor("ident", [128, 128], BF16, kind="ExternalInput").ap()
    triu_in = nc.dram_tensor("triu", [C, C], BF16, kind="ExternalInput").ap()

    out_hbm = nc.dram_tensor("out", [T_SEQ, D], F32, kind="ExternalOutput").ap()
    o_spill = nc.dram_tensor("o_spill", [H * 128, T_SEQ], BF16, kind="Internal").ap()
    x1_dram = nc.dram_tensor("x1_dram", [T_SEQ, D], BF16, kind="Internal").ap()
    wcast = nc.dram_tensor("wcast", [5 * D, D], BF16, kind="Internal").ap()

    dbg = {}
    if _dbg("yT"):
        dbg["d_yT"] = nc.dram_tensor("d_yT", [128, NCH * T_SEQ], BF16, kind="ExternalOutput").ap()
    if _dbg("tabs"):
        dbg["d_tabS"] = nc.dram_tensor("d_tabS", [128, T_SEQ], BF16, kind="ExternalOutput").ap()
        dbg["d_tabC"] = nc.dram_tensor("d_tabC", [128, T_SEQ], BF16, kind="ExternalOutput").ap()
    if _dbg("kv"):
        dbg["d_Kc"] = nc.dram_tensor("d_Kc", [128, T_SEQ], BF16, kind="ExternalOutput").ap()
        dbg["d_Ks"] = nc.dram_tensor("d_Ks", [128, T_SEQ], BF16, kind="ExternalOutput").ap()
        dbg["d_Vt"] = nc.dram_tensor("d_Vt", [128, T_SEQ], BF16, kind="ExternalOutput").ap()
    if _dbg("oh"):
        dbg["d_Oh"] = nc.dram_tensor("d_Oh", [128, T_SEQ], BF16, kind="ExternalOutput").ap()
    if _dbg("att"):
        dbg["d_KVt"] = nc.dram_tensor("d_KVt", [128, 256], BF16, kind="ExternalOutput").ap()
        dbg["d_Sarr"] = nc.dram_tensor("d_Sarr", [128, NCH * 256], BF16, kind="ExternalOutput").ap()
        dbg["d_Qt"] = nc.dram_tensor("d_Qt", [128, 1024], BF16, kind="ExternalOutput").ap()
        dbg["d_Am"] = nc.dram_tensor("d_Am", [128, 4 * C], BF16, kind="ExternalOutput").ap()
        dbg["d_mask"] = nc.dram_tensor("d_mask", [128, 4 * C], BF16, kind="ExternalOutput").ap()
    if _dbg("x1"):
        dbg["d_x1"] = nc.dram_tensor("d_x1", [T_SEQ, D], BF16, kind="ExternalOutput").ap()
    if _dbg("y1T"):
        dbg["d_y1T"] = nc.dram_tensor("d_y1T", [128, NCH * T_SEQ], BF16, kind="ExternalOutput").ap()

    if True:
        e = lambda t: t

        yT = e(nc.alloc_sbuf_tensor("yT", [128, NCH * T_SEQ], BF16))
        xbuf = e(nc.alloc_sbuf_tensor("xbuf", [128, 2 * D], F32))
        ybuf = e(nc.alloc_sbuf_tensor("ybuf", [128, 2 * D], BF16))
        x1b = e(nc.alloc_sbuf_tensor("x1b", [128, 2 * D], BF16))
        y1b = e(nc.alloc_sbuf_tensor("y1b", [128, 2 * D], BF16))
        x1l = e(nc.alloc_sbuf_tensor("x1l", [128, 2 * D], BF16))
        outb = e(nc.alloc_sbuf_tensor("outb", [128, 2 * D], F32))
        o2b = e(nc.alloc_sbuf_tensor("o2b", [128, 2 * C], BF16))
        stat6 = e(nc.alloc_sbuf_tensor("stat6", [128, 24], F32))
        mv = e(nc.alloc_sbuf_tensor("mv", [128, 4], F32))
        rstd = e(nc.alloc_sbuf_tensor("rstd", [128, 2], F32))
        sqv = e(nc.alloc_sbuf_tensor("sqv", [128, 2], F32))
        b2 = e(nc.alloc_sbuf_tensor("b2", [128, 2], F32))
        prm = e(nc.alloc_sbuf_tensor("prm", [128, PCOLS], F32))
        es1 = e(nc.alloc_sbuf_tensor("es1", [128, H], F32))
        cst = e(nc.alloc_sbuf_tensor("cst", [128, len(CONST_VALS)], F32))
        onescol = e(nc.alloc_sbuf_tensor("onescol", [128, 2], BF16))
        rowsbf = e(nc.alloc_sbuf_tensor("rowsbf", [1, ROWS_N], BF16))
        ident = e(nc.alloc_sbuf_tensor("ident_sb", [128, 128], BF16))
        w_all = e(nc.alloc_sbuf_tensor("w_all", [128, (LEV + 1) * H], F32))
        thr_all = e(nc.alloc_sbuf_tensor("thr_all", [128, LEV * H], F32))
        srow = e(nc.alloc_sbuf_tensor("srow", [1, C], BF16))
        rsb = e(nc.alloc_sbuf_tensor("rsb", [128, C], BF16))
        rsrowb = rsb[0:1, :]

        for i, v in enumerate(CONST_VALS):
            nc.const_aps.register(F32, v, cst[:, i:i + 1])

        A1 = nc.auto_arena()
        treeT = A1.sbuf_tensor("treeT", [128, T_SEQ], F32)
        tabS = A1.sbuf_tensor("tabS", [128, T_SEQ], BF16)
        tabC = A1.sbuf_tensor("tabC", [128, T_SEQ], BF16)
        Kc = A1.sbuf_tensor("Kc", [128, T_SEQ], BF16)
        Ks = A1.sbuf_tensor("Ks", [128, T_SEQ], BF16)
        Vtok = A1.sbuf_tensor("Vtok", [128, T_SEQ], BF16)
        Ktok = A1.sbuf_tensor("Ktok", [128, 2048], BF16)
        Qt = A1.sbuf_tensor("Qt", [128, 2048], BF16)
        psil = A1.sbuf_tensor("psil", [128, 2 * C], BF16)
        Am = A1.sbuf_tensor("Am", [128, 8 * C], BF16)
        maskh = A1.sbuf_tensor("maskh", [128, 128], BF16)
        KVall = A1.sbuf_tensor("KVall", [128, NCH * 256], BF16)
        KVtot = A1.sbuf_tensor("KVtot", [128, 256], BF16)
        Sfp = A1.sbuf_tensor("Sfp", [128, 256], F32)
        Sarr = A1.sbuf_tensor("Sarr", [128, NCH * 256], BF16)
        Oh = A1.sbuf_tensor("Oh", [128, 2 * C], BF16)
        wq_h = A1.sbuf_tensor("wq_h", [128, 2 * 1024], BF16)
        wk_h = A1.sbuf_tensor("wk_h", [128, 2 * 1024], BF16)
        wv_h = A1.sbuf_tensor("wv_h", [128, 2 * 1024], BF16)

        A2 = nc.auto_arena()
        WuT = A2.sbuf_tensor("WuT", [128, NCH * D], BF16)
        WoT = A2.sbuf_tensor("WoT", [128, NCH * D], BF16)
        Oslc = A2.sbuf_tensor("Oslc", [128, 2 * H * C], BF16)
        ubuf = A2.sbuf_tensor("ubuf", [128, 2 * H * C], BF16)

        A3 = nc.auto_arena()
        WrT = A3.sbuf_tensor("WrT", [128, NCH * D], BF16)
        WkT = A3.sbuf_tensor("WkT", [128, NCH * D], BF16)
        WvT = A3.sbuf_tensor("WvT", [128, NCH * D], BF16)
        xrb = A3.sbuf_tensor("xrb", [128, 2 * NCH * C], BF16)
        xkb = A3.sbuf_tensor("xkb", [128, 2 * NCH * C], BF16)
        kact = A3.sbuf_tensor("kact", [128, NCH * C], BF16)
        rbuf = A3.sbuf_tensor("rbuf", [128, 2 * D], BF16)
        fsb = A3.sbuf_tensor("fsb", [128, 2 * D], F32)

        base0 = nc.sbuf_base
        A1.allocate()
        top1 = nc.sbuf_base
        nc.sbuf_base = base0
        A2.allocate()
        top2 = nc.sbuf_base
        nc.sbuf_base = base0
        A3.allocate()
        top3 = nc.sbuf_base
        nc.sbuf_base = max(top1, top2, top3)

        psA = e(nc.alloc_psum_tensor("psA", [128, C], F32))
        psB = e(nc.alloc_psum_tensor("psB", [128, C], F32))
        psC = e(nc.alloc_psum_tensor("psC", [128, C], F32))
        psO = e(nc.alloc_psum_tensor("psO", [128, C], F32))
        psKt = e(nc.alloc_psum_tensor("psKt", [128, C], F32))
        psKo = e(nc.alloc_psum_tensor("psKo", [128, C], F32))
        psMS = e(nc.alloc_psum_tensor("psMS", [128, C], F32))
        psT = e(nc.alloc_psum_tensor("psT", [128, 1024], BF16))

        p = Prog(nc)
        sL = p.sem("sL"); sG = p.sem("sG"); sP = p.sem("sP"); sO2 = p.sem("sO2")
        sV = p.sem("sV"); sS = p.sem("sS"); sST = p.sem("sST")
        sX = [p.sem("sX0"), p.sem("sX1")]
        sOs = [p.sem("sOs0"), p.sem("sOs1")]
        sX1L = [p.sem("sX1L0"), p.sem("sX1L1")]
        sGp = p.sem("sGp")
        sGw = [p.sem("sGw0"), p.sem("sGw1")]
        sGm = p.sem("sGm")
        sGw2 = p.sem("sGw2")
        sGc3 = p.sem("sGc3")
        sGc4 = p.sem("sGc4")
        sGw3 = p.sem("sGw3"); sGw4 = p.sem("sGw4"); sGw5 = p.sem("sGw5")
        sSTo = [p.sem("sSTo0"), p.sem("sSTo1")]
        sSTx = [p.sem("sSTx0"), p.sem("sSTx1")]
        sSTu = [p.sem("sSTu0"), p.sem("sSTu1")]

        ones_row = rowsbf[0:1, 2 * D:2 * D + 128]

        def mm(out, lhsT, rhs, start, stop, inc=None):
            return p.emit("tensor", lambda t, o=out, l=lhsT, r=rhs, a=start, b=stop:
                          t.matmul(o, lhsT=l, rhs=r, start=a, stop=b), inc=inc)

        def tr(out, in_, inc=None):
            return p.emit("tensor", lambda t, o=out, i=in_:
                          t.transpose(o, in_=i, identity=ident[:]), inc=inc)

        def act(out, in_, func, bias=0.0, scale=1.0, accum=None, inc=None):
            return p.emit("scalar", lambda s, o=out, i=in_, f=func, b=bias, sc=scale, a=accum:
                          s.activation(o, i, f, bias=b, scale=sc, accum_out=a), inc=inc)

        def vop(fn, inc=None):
            return p.emit("vector", fn, inc=inc)

        def ln_stats(src0, src1, slot=0):
            st = stat6[:, slot * 12:(slot + 1) * 12]
            mvs = mv[:, slot * 2:(slot + 1) * 2]
            sq = sqv[:, slot:slot + 1]
            rs = rstd[:, slot:slot + 1]
            vop(lambda v, a=src0, st=st: v.bn_stats(st[:, 0:6], a))
            vop(lambda v, a=src1, st=st: v.bn_stats(st[:, 6:12], a), inc=sV)
            p.wait("vector", sV)
            vop(lambda v, st=st, mvs=mvs: v.bn_aggr(mvs, st), inc=sV)
            p.wait("scalar", sV)
            act(sq, mvs[:, 1:2], AF.Sqrt, bias=EPS_LN, inc=sS)
            p.wait("vector", sS)
            vop(lambda v, rs=rs, sq=sq: v.reciprocal(rs, sq), inc=sV)
            p.wait("vector", sV)

        p.dma("gpsimd", prm[:], params, sGp)
        p.dma("gpsimd", cst[:], consts_in, sGp)
        p.dma("gpsimd", onescol[:], cones_in, sGp)
        p.dma("gpsimd", rowsbf[:], rows_in, sGp)
        p.dma("gpsimd", ident[:], ident_in, sGp)

        def load_head_weights(h):
            for w, off in ((wq_h, 0), (wk_h, D), (wv_h, 2 * D)):
                src = wqkvuT[:, off + h * HD: off + (h + 1) * HD]
                src = src.rearrange("(c p) n -> p c n", p=128)
                dst = w[:, (h % 2) * 1024:(h % 2 + 1) * 1024].rearrange(
                    "p (c n) -> p c n", c=NCH)
                p.dma("gpsimd", dst, src, sGw[h % 2])
        load_head_weights(0)
        g_head_w = {0: p.counts["sGw0"]}

        p.wait("vector", sGp)
        vop(lambda v: v.tensor_copy(w_all[:, 0:H], prm[:, PC_THETA:PC_THETA + H]), inc=sV)
        p.wait("vector", sV)
        for k in range(LEV):
            wk = w_all[:, k * H:(k + 1) * H]
            wn = w_all[:, (k + 1) * H:(k + 2) * H]
            thr = thr_all[:, k * H:(k + 1) * H]
            vop(lambda v, a=thr, b=wk: v.tensor_scalar(a, b, -1.0, TWO_PI, ALU.mult, ALU.add))
            vop(lambda v, a=wn, b=wk: v.tensor_scalar(a, b, PI, -TWO_PI, ALU.is_ge, ALU.mult), inc=sV)
            p.wait("vector", sV)
            vop(lambda v, a=wn, b=wk: v.scalar_tensor_tensor(a, b, 2.0, a, ALU.mult, ALU.add), inc=sV)
            p.wait("vector", sV)
        wchain_done = p.counts["sV"]

        p.wait("scalar", sGp)
        act(es1[:], prm[:, PC_SLOPE:PC_SLOPE + H], AF.Exp, inc=sS)
        p.wait("vector", sS)
        vop(lambda v: v.tensor_scalar(es1[:], es1[:], -1.0, None, ALU.add), inc=sV)
        es1_ready = p.counts["sV"]

        last_x_cons = {0: 0, 1: 0}
        s1_ap = {-2: 0, -1: 0}
        s1_tr = {-2: 0, -1: 0}
        s1_ev = {-1: 0}
        s1_xl = {}
        s1_ag = {}
        s1_sq = {}

        def s1_load(t):
            b = t % 2
            p.wait("sync", sV, s1_ap[t - 2])
            p.dma("sync", xbuf[:, b * D:(b + 1) * D], x_in[t * 128:(t + 1) * 128, :], sX[b])
            s1_xl[t] = (sX[b], p.counts[sX[b]])

        def s1_stats(t):
            b = t % 2
            xsl = xbuf[:, b * D:(b + 1) * D]
            st = stat6[:, b * 12:(b + 1) * 12]
            p.wait("vector", *s1_xl[t])
            vop(lambda v, a=xsl, st=st: v.bn_stats(st[:, 0:6], a[:, 0:512]))
            vop(lambda v, a=xsl, st=st: v.bn_stats(st[:, 6:12], a[:, 512:1024]), inc=sV)
            p.wait("vector", sV)
            vop(lambda v, st=st, b=b: v.bn_aggr(mv[:, b * 2:(b + 1) * 2], st), inc=sV)
            s1_ag[t] = p.counts["sV"]

        def s1_sqrt(t):
            b = t % 2
            p.wait("scalar", sV, s1_ag[t])
            p.wait("scalar", sV, s1_ap[t - 2])
            act(sqv[:, b:b + 1], mv[:, b * 2 + 1:b * 2 + 2], AF.Sqrt, bias=EPS_LN, inc=sS)
            s1_sq[t] = p.counts["sS"]

        def s1_apply(t):
            b = t % 2
            xsl = xbuf[:, b * D:(b + 1) * D]
            p.wait("vector", sS, s1_sq[t])
            vop(lambda v, b=b: v.reciprocal(rstd[:, b:b + 1], sqv[:, b:b + 1]), inc=sV)
            p.wait("vector", sV)
            p.wait("vector", sP, s1_tr[t - 2])
            vop(lambda v, a=xsl, b=b: v.tensor_scalar(
                ybuf[:, b * D:(b + 1) * D], a, mv[:, b * 2:b * 2 + 1], rstd[:, b:b + 1],
                ALU.subtract, ALU.mult), inc=sV)
            s1_ap[t] = p.counts["sV"]
            last_x_cons[b] = s1_ap[t]

        def s1_tr_f(t):
            b = t % 2
            p.wait("tensor", sV, s1_ap[t])
            p.wait("tensor", sV, s1_ev[t - 1])
            for j in range(NCH):
                tr(psT[:, j * 128:(j + 1) * 128],
                   ybuf[:, b * D + j * 128: b * D + (j + 1) * 128],
                   inc=(sP if j == NCH - 1 else None))
            s1_tr[t] = p.counts["sP"]

        def s1_ev_f(t):
            p.wait("vector", sP, s1_tr[t])
            dst = yT[:].rearrange("p (c n) -> p c n", c=NCH)[:, :, t * 128:(t + 1) * 128]
            vop(lambda v, d=dst: v.tensor_copy(
                d, psT[:].rearrange("p (c n) -> p c n", c=NCH)), inc=sV)
            s1_ev[t] = p.counts["sV"]

        s1_load(0)
        s1_load(1)
        s1_stats(0)
        s1_sqrt(0)
        for t in range(NT):
            if t + 1 < NT:
                s1_stats(t + 1)
                s1_sqrt(t + 1)
            s1_apply(t)
            s1_tr_f(t)
            if t + 2 < NT:
                s1_load(t + 2)
            s1_ev_f(t)

        yT_done = p.counts["sV"]
        if _dbg("yT"):
            p.wait("sync", sV, yT_done)
            p.dma("sync", dbg["d_yT"], yT[:], sST)

        tabs_c = {}

        def tables_stage1(h):
            vop(lambda v: v.memset(treeT[:, 0:1], 0.0), inc=sV)
            p.wait("vector", sV)
            for k in range(10):
                wk = w_all[:, k * H + h:k * H + h + 1]
                thr = thr_all[:, k * H + h:k * H + h + 1]
                w_ = 1 << k
                vop(lambda v, w=w_, t=thr: v.tensor_scalar(
                    treeT[:, w:2 * w], treeT[:, 0:w], t, -TWO_PI, ALU.is_ge, ALU.mult), inc=sV)
                p.wait("vector", sV)
                vop(lambda v, w=w_, wp=wk: v.scalar_tensor_tensor(
                    treeT[:, w:2 * w], treeT[:, 0:w], wp, treeT[:, w:2 * w], ALU.add, ALU.add), inc=sV)
                p.wait("vector", sV)
            tables_region(h, 0, 1024, (0, 1))

        def tables_stage2(h):
            k = 10
            wk = w_all[:, k * H + h:k * H + h + 1]
            thr = thr_all[:, k * H + h:k * H + h + 1]
            w_ = 1 << k
            vop(lambda v, w=w_, t=thr: v.tensor_scalar(
                treeT[:, w:2 * w], treeT[:, 0:w], t, -TWO_PI, ALU.is_ge, ALU.mult), inc=sV)
            p.wait("vector", sV)
            vop(lambda v, w=w_, wp=wk: v.scalar_tensor_tensor(
                treeT[:, w:2 * w], treeT[:, 0:w], wp, treeT[:, w:2 * w], ALU.add, ALU.add), inc=sV)
            p.wait("vector", sV)
            tables_region(h, 1024, 2048, (2, 3))

        def tables_stage3(h):
            k = 11
            wk = w_all[:, k * H + h:k * H + h + 1]
            thr = thr_all[:, k * H + h:k * H + h + 1]
            w_ = 1 << k
            vop(lambda v, w=w_, t=thr: v.tensor_scalar(
                treeT[:, w:2 * w], treeT[:, 0:w], t, -TWO_PI, ALU.is_ge, ALU.mult), inc=sV)
            p.wait("vector", sV)
            vop(lambda v, w=w_, wp=wk: v.scalar_tensor_tensor(
                treeT[:, w:2 * w], treeT[:, 0:w], wp, treeT[:, w:2 * w], ALU.add, ALU.add), inc=sV)
            p.wait("vector", sV)
            tables_region(h, 2048, 4096, (4, 5, 6, 7))

        def tables_region(h, a, b, chunks):
            vop(lambda v, a=a, b=b: v.tensor_scalar(
                treeT[:, a:b], treeT[:, a:b], 1e-6, TWO_PI - 1e-6, ALU.max, ALU.min), inc=sV)
            tv = p.counts["sV"]
            p.wait("scalar", sV, tv)
            act(tabS[:, a:b], treeT[:, a:b], AF.Sin, bias=-PI, inc=sS)
            p.wait("vector", sS)
            vop(lambda v, a=a, b=b: v.tensor_scalar(
                xbuf[:, 0:b - a], treeT[:, a:b], 1.5 * PI, -TWO_PI, ALU.is_ge, ALU.mult), inc=sV)
            p.wait("vector", sV)
            vop(lambda v, a=a, b=b: v.tensor_add(
                xbuf[:, 0:b - a], xbuf[:, 0:b - a], treeT[:, a:b]), inc=sV)
            p.wait("vector", sV)
            vop(lambda v, a=a, b=b: v.tensor_scalar(
                xbuf[:, 0:b - a], xbuf[:, 0:b - a], -PI / 2 + 1e-6, 1.5 * PI - 1e-6,
                ALU.max, ALU.min), inc=sV)
            p.wait("scalar", sV)
            act(tabC[:, a:b], xbuf[:, 0:b - a], AF.Sin, bias=-PI / 2, inc=sS)
            for c in chunks:
                tabs_c[c] = p.counts["sS"]

        prev_mask_cons = 0
        oh_spilled = {0: 0, 1: 0}
        prev_passB_P = 0
        prev_head_V = 0
        prev_qev6 = 0
        for h in range(H):
            hb = h % 2
            if h + 1 < H:
                p.wait("gpsimd", sP, prev_passB_P)
                load_head_weights(h + 1)
                g_head_w[h + 1] = p.counts["sGw0" if (h + 1) % 2 == 0 else "sGw1"]
            pre_tab_S = p.counts["sS"]
            pre_tab_V = p.counts["sV"]
            if h == 0:
                p.wait("vector", sV, wchain_done)
                tables_stage1(0)

            p.wait("gpsimd", sV, prev_mask_cons)
            p.dma("gpsimd", maskh[:], triu_in[0:128, 0:128], sGm)
            mtr = p.counts["sGm"]
            if h < 5:
                csrc = [wqkvuT[:, 3 * D:4 * D], woT_in, wrT_in, wkT_in, wvT_in][h]
                p.dma("gpsimd", wcast[h * D:(h + 1) * D, :], csrc,
                      sGc3 if h < 2 else sGc4)
            p.wait("vector", sGm, mtr)
            p.wait("vector", sV, es1_ready)
            vop(lambda v, hh=h: v.tensor_scalar(
                maskh[:], maskh[:], es1[:, hh:hh + 1], None, ALU.mult), inc=sV)
            mask_ready = p.counts["sV"]

            p.wait("tensor", sGw[h % 2], g_head_w[h])
            if h == 0:
                p.wait("tensor", sGp)
            kb = prm[:, PC_QKVU_B + 8 + h:PC_QKVU_B + 8 + h + 1]
            qb = prm[:, PC_QKVU_B + h:PC_QKVU_B + h + 1]

            startS_A = p.counts["sS"] if h == 0 else pre_tab_S
            startV_A = p.counts["sV"] if h == 0 else pre_tab_V
            kev_d = {-1: p.counts["sS"]}
            klr_d = {-1: prev_head_V, -2: prev_head_V}
            vev_d = {}
            kte_d = {}
            kvo_d = {-1: p.counts["sS"]}
            ka_p = {}
            vp_p = {}
            ktr_p = {}
            kv_p = {-1: 0}

            def ka(c):
                bank = psA if c % 2 == 0 else psC
                p.wait("tensor", sS, kev_d[c - 2] if c >= 2 else startS_A)
                for dc in range(NCH):
                    mm(bank[:], wk_h[:, hb * 1024 + dc * 128: hb * 1024 + (dc + 1) * 128],
                       yT[:, dc * T_SEQ + c * C: dc * T_SEQ + (c + 1) * C],
                       start=(dc == 0), stop=(dc == NCH - 1),
                       inc=(sP if dc == NCH - 1 else None))
                ka_p[c] = p.counts["sP"]

            def kev(c):
                bank = psA if c % 2 == 0 else psC
                p.wait("scalar", sP, ka_p[c])
                p.wait("scalar", sV, klr_d[c - 2])
                act(psil[:, (c % 2) * C:(c % 2 + 1) * C], bank[:], AF.Silu, bias=kb, inc=sS)
                kev_d[c] = p.counts["sS"]

            def klr(c):
                p.wait("vector", sS, kev_d[c])
                p.wait("vector", sS, tabs_c[c])
                ps = psil[:, (c % 2) * C:(c % 2 + 1) * C]
                vop(lambda v, c=c, ps=ps: v.tensor_mul(
                    Kc[:, c * C:(c + 1) * C], ps, tabC[:, c * C:(c + 1) * C]))
                vop(lambda v, c=c, ps=ps: v.tensor_mul(
                    Ks[:, c * C:(c + 1) * C], ps, tabS[:, c * C:(c + 1) * C]), inc=sV)
                klr_d[c] = p.counts["sV"]

            def vproj(c):
                bank = psB if c % 2 == 0 else psO
                p.wait("tensor", sS, vev_d[c - 2] if c >= 2 else startS_A)
                if c < 2:
                    p.wait("tensor", sV, startV_A)
                for j in range(4):
                    for dc in range(NCH):
                        mm(bank[:, j * 128:(j + 1) * 128],
                           yT[:, dc * T_SEQ + c * C + j * 128: dc * T_SEQ + c * C + (j + 1) * 128],
                           wv_h[:, hb * 1024 + dc * 128: hb * 1024 + (dc + 1) * 128],
                           start=(dc == 0), stop=False)
                    mm(bank[:, j * 128:(j + 1) * 128], ones_row,
                       rowsbf[0:1, h * HD: (h + 1) * HD],
                       start=False, stop=True, inc=(sP if j == 3 else None))
                vp_p[c] = p.counts["sP"]

            def vev(c):
                bank = psB if c % 2 == 0 else psO
                p.wait("scalar", sP, vp_p[c])
                act(Vtok[:, c * C:(c + 1) * C], bank[:], AF.Identity, inc=sS)
                vev_d[c] = p.counts["sS"]

            def ktr(c):
                p.wait("tensor", sV, klr_d[c])
                if c == 0:
                    p.wait("tensor", sV, startV_A)
                p.wait("tensor", sS, kte_d[c - 1] if c >= 1 else startS_A)
                for half, srcb in ((0, Kc), (1, Ks)):
                    for j in range(4):
                        tr(psT[:, half * C + j * 128: half * C + (j + 1) * 128],
                           srcb[:, c * C + j * 128: c * C + (j + 1) * 128],
                           inc=(sP if (half, j) == (1, 3) else None))
                ktr_p[c] = p.counts["sP"]

            def kte(c):
                p.wait("scalar", sP, ktr_p[c])
                act(Ktok[:, (c % 2) * 1024:(c % 2 + 1) * 1024], psT[:], AF.Identity, inc=sS)
                kte_d[c] = p.counts["sS"]

            def kv(c):
                p.wait("tensor", sS, vev_d[c])
                p.wait("tensor", sS, kte_d[c])
                p.wait("tensor", sS, kvo_d[c - 1])
                if c == 0:
                    p.wait("tensor", sV, startV_A)
                kb2 = (c % 2) * 1024
                for half in range(2):
                    for j in range(4):
                        mm(psKt[:, half * 128:(half + 1) * 128],
                           Ktok[:, kb2 + half * C + j * 128: kb2 + half * C + (j + 1) * 128],
                           Vtok[:, c * C + j * 128: c * C + (j + 1) * 128],
                           start=(c == 0 and half == 0 and j == 0),
                           stop=(c == NCH - 1 and half == 1 and j == 3))
                for half in range(2):
                    for j in range(4):
                        mm(psKo[:, half * 128:(half + 1) * 128],
                           Ktok[:, kb2 + half * C + j * 128: kb2 + half * C + (j + 1) * 128],
                           Vtok[:, c * C + j * 128: c * C + (j + 1) * 128],
                           start=(half == 0 and j == 0) if True else (j == 0),
                           stop=(half == 1 and j == 3),
                           inc=(sP if (half, j) == (1, 3) else None))
                kv_p[c] = p.counts["sP"]

            def kvo(c):
                p.wait("scalar", sP, kv_p[c])
                act(KVall[:, c * 256:(c + 1) * 256], psKo[:, 0:256], AF.Identity,
                    scale=es1[:, h:h + 1], inc=sS)
                kvo_d[c] = p.counts["sS"]

            ka(0)
            ka(1)
            kev(0)
            klr(0)
            vproj(0)
            tables_stage2(h)
            ktr(0)
            kte(0)
            for c in range(NCH):
                if c + 2 < NCH:
                    ka(c + 2)
                if c + 1 < NCH:
                    kev(c + 1)
                    klr(c + 1)
                    vproj(c + 1)
                vev(c)
                if c + 1 < NCH:
                    ktr(c + 1)
                    kte(c + 1)
                kv(c)
                kvo(c)
                if c == 0:
                    tables_stage3(h)

            p.wait("scalar", sP, kv_p[NCH - 1])
            act(KVtot[:], psKt[:, 0:256], AF.Identity, inc=sS)
            kvtot_done = p.counts["sS"]

            if _dbg("tabs") and h == 0:
                p.wait("sync", sS, tabs_ready)
                p.dma("sync", dbg["d_tabS"], tabS[:], sST)
                p.dma("sync", dbg["d_tabC"], tabC[:], sST)
            if _dbg("kv") and h == 0:
                p.wait("sync", sV)
                p.wait("sync", sS)
                p.dma("sync", dbg["d_Kc"], Kc[:], sST)
                p.dma("sync", dbg["d_Ks"], Ks[:], sST)
                p.dma("sync", dbg["d_Vt"], Vtok[:], sST)

            startS_B = p.counts["sS"]
            startP_B = p.counts["sP"]
            qev_d = {}
            qlr_d = {-1: p.counts["sV"], -2: p.counts["sV"]}
            qev_d[-2] = 0; qev_d[-1] = 0
            qa_p = {}
            ab_p = {}
            am_d = {}
            am_s = {}
            o_p = {}
            oev_d = {-1: p.counts["sS"]}
            abanks = [psB, psC, psO, psKt]

            def qa(c):
                bank = psA if c % 2 == 0 else psMS
                p.wait("tensor", sS, qev_d[c - 2] if c >= 2
                       else (kev_d[6] if c == 0 else startS_A))
                for dc in range(NCH):
                    mm(bank[:], wq_h[:, hb * 1024 + dc * 128: hb * 1024 + (dc + 1) * 128],
                       yT[:, dc * T_SEQ + c * C: dc * T_SEQ + (c + 1) * C],
                       start=(dc == 0), stop=(dc == NCH - 1),
                       inc=(sP if dc == NCH - 1 else None))
                qa_p[c] = p.counts["sP"]

            def qev(c):
                bank = psA if c % 2 == 0 else psMS
                p.wait("scalar", sP, qa_p[c])
                p.wait("scalar", sV, qlr_d[c - 2])
                act(psil[:, (c % 2) * C:(c % 2 + 1) * C], bank[:], AF.Silu, bias=qb, inc=sS)
                qev_d[c] = p.counts["sS"]

            def qlr(c):
                p.wait("vector", sS, qev_d[c])
                p.wait("vector", sP, o_p.get(c - 2, startP_B))
                qs = (c % 2) * 1024
                ps = psil[:, (c % 2) * C:(c % 2 + 1) * C]
                vop(lambda v, c=c, ps=ps, qs=qs: v.tensor_mul(
                    Qt[:, qs:qs + C], ps, tabC[:, c * C:(c + 1) * C]))
                vop(lambda v, c=c, ps=ps, qs=qs: v.tensor_mul(
                    Qt[:, qs + C:qs + 2 * C], ps, tabS[:, c * C:(c + 1) * C]), inc=sV)
                qlr_d[c] = p.counts["sV"]

            def ab(c):
                qs = (c % 2) * 1024
                p.wait("tensor", sV, qlr_d[c])
                for j in range(4):
                    bank = abanks[j]
                    nj = (4 - j) * 128
                    if c == 0:
                        p.wait("tensor", sS,
                               [vev_d[6], kev_d[7], vev_d[7], kvtot_done][j])
                    p.wait("tensor", sV, am_d.get((c - 1, j), 0))
                    p.wait("tensor", sS, am_s.get((c - 1, j), 0))
                    mm(bank[:, 0:nj], Kc[:, c * C + j * 128: c * C + (j + 1) * 128],
                       Qt[:, qs + j * 128: qs + C], start=True, stop=False)
                    mm(bank[:, 0:nj], Ks[:, c * C + j * 128: c * C + (j + 1) * 128],
                       Qt[:, qs + C + j * 128: qs + 2 * C], start=False, stop=True, inc=sP)
                    ab_p[(c, j)] = p.counts["sP"]

            def amev(c):
                ab2 = (c % 2) * 4 * C
                for j in range(4):
                    p.wait("vector", sP, ab_p[(c, j)])
                    if c == 0:
                        p.wait("vector", sV, mask_ready)
                    bank = abanks[j]
                    vop(lambda v, j=j, bk=bank, ab2=ab2: v.tensor_mul(
                        Am[:, ab2 + j * C: ab2 + j * C + 128], bk[:, 0:128],
                        maskh[:]), inc=sV)
                    am_d[(c, j)] = p.counts["sV"]
                    if j < 3:
                        nj = (4 - j) * 128
                        if TRI_SCALAR_EVAC:
                            p.wait("scalar", sP, ab_p[(c, j)])
                            act(Am[:, ab2 + j * C + 128: ab2 + j * C + nj],
                                bank[:, 128:nj], AF.Identity, scale=es1[:, h:h + 1], inc=sS)
                            am_s[(c, j)] = p.counts["sS"]
                        else:
                            vop(lambda v, j=j, bk=bank, ab2=ab2, nj=nj, hh=h: v.tensor_scalar(
                                Am[:, ab2 + j * C + 128: ab2 + j * C + nj],
                                bk[:, 128:nj], es1[:, hh:hh + 1], None, ALU.mult), inc=sV)
                            am_d[(c, j)] = p.counts["sV"]

            def omm(c):
                ab2 = (c % 2) * 4 * C
                qs = (c % 2) * 1024
                p.wait("tensor", sV, am_d[(c, 3)])
                if TRI_SCALAR_EVAC:
                    p.wait("tensor", sS, am_s[(c, 2)])
                else:
                    p.wait("tensor", sV, am_d[(c, 2)])
                p.wait("tensor", sS, oev_d[c - 1])
                p.wait("tensor", sS, kvtot_done)
                p.wait("tensor", sV, sarr_c[c])
                for j in range(4):
                    nj = (4 - j) * 128
                    mm(psKo[:, j * 128:C], Vtok[:, c * C + j * 128: c * C + (j + 1) * 128],
                       Am[:, ab2 + j * C: ab2 + j * C + nj], start=(j == 0), stop=False)
                for half in range(2):
                    mm(psKo[:], KVtot[:, half * 128:(half + 1) * 128],
                       Qt[:, qs + half * C: qs + (half + 1) * C], start=False, stop=False)
                for half in range(2):
                    mm(psKo[:], Sarr[:, c * 256 + half * 128: c * 256 + (half + 1) * 128],
                       Qt[:, qs + half * C: qs + (half + 1) * C], start=False, stop=(half == 1),
                       inc=(sP if half == 1 else None))
                o_p[c] = p.counts["sP"]

            def oev(c):
                p.wait("scalar", sP, o_p[c])
                ohb = c % 2
                p.wait("scalar", sSTo[ohb], oh_spilled[ohb])
                act(Oh[:, ohb * C:(ohb + 1) * C], psKo[:], AF.Identity, inc=sS)
                oev_d[c] = p.counts["sS"]
                p.wait("sync", sS, oev_d[c])
                p.dma("sync", o_spill[h * 128:(h + 1) * 128, c * C:(c + 1) * C],
                      Oh[:, ohb * C:(ohb + 1) * C], sSTo[ohb])
                oh_spilled[ohb] = p.counts["sSTo0" if ohb == 0 else "sSTo1"]

            qa(0)
            qev(0)
            qlr(0)
            p.wait("vector", sS, kvo_d[NCH - 1])
            p.wait("vector", sP, prev_passB_P)
            sarr_c = {}
            vop(lambda v: v.memset(Sfp[:], 0.0), inc=sV)
            p.wait("vector", sV)
            for c in range(NCH):
                vop(lambda v, c=c: v.tensor_copy(Sarr[:, c * 256:(c + 1) * 256], Sfp[:]), inc=sV)
                sarr_c[c] = p.counts["sV"]
                p.wait("vector", sV)
                if c < NCH - 1:
                    def _pfx(v, c=c):
                        with nc.allow_low_precision("fp32 accumulator over bf16 KV'' terms"):
                            return v.tensor_add(Sfp[:], Sfp[:], KVall[:, c * 256:(c + 1) * 256])
                    vop(_pfx, inc=sV)
                    p.wait("vector", sV)

            qa(1)
            ab(0)
            amev(0)
            for c in range(NCH):
                if c + 1 < NCH:
                    qev(c + 1)
                    qlr(c + 1)
                if c + 2 < NCH:
                    qa(c + 2)
                if c + 1 < NCH:
                    ab(c + 1)
                omm(c)
                if c + 1 < NCH:
                    amev(c + 1)
                oev(c)
                if c == 1 and h + 1 < H:
                    tables_stage1(h + 1)
                if h == H - 1 and c == 0:
                    p.wait("sync", sS, tabs_c[NCH - 1])
                    p.wait("sync", sV, qlr_d[1])
                    p.wait("sync", sGc3)
                    p.dma("sync", WuT[:].rearrange("p (c n) -> p c n", c=NCH),
                          wcast[0:D, :].rearrange("(c p) n -> p c n", p=128), sGw2)
                if h == H - 1 and c == 6:
                    p.wait("sync", sV, qlr_d[7])
                    p.wait("sync", sP, ab_p[(7, 3)])
                    p.dma("sync", WoT[:].rearrange("p (c n) -> p c n", c=NCH),
                          wcast[D:2 * D, :].rearrange("(c p) n -> p c n", p=128), sGw2)
                    p.wait("sync", sSTo[0])
                    for hh in range(H):
                        p.dma("sync", Oslc[:, hh * C:(hh + 1) * C],
                              o_spill[hh * 128:(hh + 1) * 128, 0:C], sOs[0])
                    early_osl0 = (sOs[0], p.counts["sOs0"])
            prev_passB_P = p.counts["sP"]
            prev_mask_cons = p.counts["sV"]
            prev_head_V = p.counts["sV"]
            prev_qev6 = qev_d[6]
            if _dbg("att") and h == 0:
                p.wait("sync", sS, kvtot_done)
                p.wait("sync", sV)
                p.dma("sync", dbg["d_KVt"], KVtot[:], sST)
                p.dma("sync", dbg["d_Sarr"], Sarr[:], sST)
                p.dma("sync", dbg["d_mask"], maskh[:], sST)
            if _dbg("oh") and h == 0:
                p.wait("sync", sSTo[0], oh_spilled[0])
                p.wait("sync", sSTo[1], oh_spilled[1])
                p.dma("sync", dbg["d_Oh"], o_spill[0:128, :], sST)

        p.wait("sync", sSTo[0])
        p.wait("sync", sSTo[1])
        p.wait("sync", sP)
        p.wait("sync", sV)
        p.wait("sync", sS)
        wuwo = p.counts["sGw2"]

        sO = p.sem("sO")
        xrk_done = {}
        prev_rh_P = {-2: 0, -1: 0}
        def pool_op(fn, inc=None):
            return p.emit("gpsimd", fn, inc=inc)

        xrk_dlt = {}

        def xrk_subs(g, ywait):
            sb = (g % 2) * NCH * C
            p.wait("gpsimd", sV, ywait)
            p.wait("gpsimd", sP, prev_rh_P.get(g - 2, 0))
            for dc in range(NCH):
                cur = yT[:, dc * T_SEQ + g * C: dc * T_SEQ + (g + 1) * C]
                if g == 0:
                    prev = yT[:, dc * T_SEQ: dc * T_SEQ + C - 1]
                    pool_op(lambda e, cur=cur, prev=prev, dc=dc: e.tensor_sub(
                        xkb[:, sb + dc * C + 1: sb + (dc + 1) * C], cur[:, 1:C], prev),
                        inc=(sO2 if dc == NCH - 1 else None))
                else:
                    prev = yT[:, dc * T_SEQ + g * C - 1: dc * T_SEQ + (g + 1) * C - 1]
                    pool_op(lambda e, cur=cur, prev=prev, dc=dc: e.tensor_sub(
                        xkb[:, sb + dc * C: sb + (dc + 1) * C], cur, prev),
                        inc=(sO2 if dc == NCH - 1 else None))
            xrk_dlt[g] = p.counts["sO2"]

        def xrk_stt(g, dcs=None):
            sb = (g % 2) * NCH * C
            p.wait("vector", sO2, xrk_dlt[g])
            p.wait("vector", sP, prev_rh_P.get(g - 2, 0))
            for dc in (range(NCH) if dcs is None else dcs):
                mur = prm[:, PC_MU_R + dc:PC_MU_R + dc + 1]
                muk = prm[:, PC_MU_K + dc:PC_MU_K + dc + 1]
                cur = yT[:, dc * T_SEQ + g * C: dc * T_SEQ + (g + 1) * C]
                if g == 0:
                    prev = yT[:, dc * T_SEQ: dc * T_SEQ + C - 1]
                    vop(lambda v, dc=dc, prev=prev, m=mur: v.scalar_tensor_tensor(
                        xrb[:, sb + dc * C + 1: sb + (dc + 1) * C],
                        xkb[:, sb + dc * C + 1: sb + (dc + 1) * C], m, prev, ALU.mult, ALU.add))
                    vop(lambda v, dc=dc, cur=cur, m=mur: v.tensor_scalar(
                        xrb[:, sb + dc * C: sb + dc * C + 1], cur[:, 0:1], m, None, ALU.mult), inc=sV)
                    p.wait("vector", sV)
                    vop(lambda v, dc=dc, prev=prev, m=muk: v.scalar_tensor_tensor(
                        xkb[:, sb + dc * C + 1: sb + (dc + 1) * C],
                        xkb[:, sb + dc * C + 1: sb + (dc + 1) * C], m, prev, ALU.mult, ALU.add))
                    vop(lambda v, dc=dc, cur=cur, m=muk: v.tensor_scalar(
                        xkb[:, sb + dc * C: sb + dc * C + 1], cur[:, 0:1], m, None, ALU.mult), inc=sV)
                    p.wait("vector", sV)
                else:
                    prev = yT[:, dc * T_SEQ + g * C - 1: dc * T_SEQ + (g + 1) * C - 1]
                    vop(lambda v, dc=dc, prev=prev, m=mur: v.scalar_tensor_tensor(
                        xrb[:, sb + dc * C: sb + (dc + 1) * C],
                        xkb[:, sb + dc * C: sb + (dc + 1) * C], m, prev, ALU.mult, ALU.add), inc=sV)
                    p.wait("vector", sV)
                    vop(lambda v, dc=dc, prev=prev, m=muk: v.scalar_tensor_tensor(
                        xkb[:, sb + dc * C: sb + (dc + 1) * C],
                        xkb[:, sb + dc * C: sb + (dc + 1) * C], m, prev, ALU.mult, ALU.add), inc=sV)
                    p.wait("vector", sV)
            xrk_done[g] = p.counts["sV"]

        def xrk_mix(g, ywait):
            xrk_subs(g, ywait)
            xrk_stt(g)


        p.wait("tensor", sGw2, wuwo)
        p.wait("tensor", sS)
        p.wait("tensor", sV)
        pairs = [(psB, psC), (psA, psO)]
        gmult_done = {-1: 0, -2: 0}
        rsb_map = {}
        srow_done = {}
        rsb_ev = {-1: 0}
        osl_map = {}
        zmm_last = {-1: 0, -2: 0}
        zl_done = {-2: 0, -1: 0}
        zmm_zp = {}
        x1st = {0: 0, 1: 0}
        y1tv = {}
        y1recip = {-2: 0, -1: 0}
        ytr_d = {-2: 0, -1: 0}
        yev_d = {-1: 0}
        aggr_z = {}
        u_kr = {}
        u_msp = {}
        u_sq_done = {}

        def oslc_load(g):
            if g == 0:
                osl_map[0] = early_osl0
                return
            p.wait("sync", sO, gmult_done[g - 1])
            p.wait("sync", sP, zmm_last[g - 2])
            base = (g % 2) * H * C
            for hh in range(H):
                p.dma("sync", Oslc[:, base + hh * C: base + (hh + 1) * C],
                      o_spill[hh * 128:(hh + 1) * 128, g * C:(g + 1) * C], sOs[g % 2])
            osl_map[g] = (sOs[g % 2], p.counts[sOs[g % 2]])

        def u_compute(g, hs):
            sl = g % 2
            osl = osl_map[g]
            pend = []

            def flush_row():
                hh0, sqc = pend.pop(0)
                p.wait("tensor", sO, sqc)
                if hh0 == 0:
                    p.wait("tensor", sV, rsb_ev[g - 1])
                mm(psMS[0:1, :], onescol[:, 0:1], o2b[:, (hh0 % 2) * C:(hh0 % 2 + 1) * C],
                   start=(hh0 == 0), stop=(hh0 == H - 1),
                   inc=(sP if hh0 == H - 1 else None))

            for hh in hs:
                bank = psKt if hh % 2 == 0 else psKo
                p.wait("tensor", sS, u_kr.get((g, hh - 2), 0))
                for dc in range(NCH):
                    mm(bank[:], WuT[:, dc * D + hh * 128: dc * D + (hh + 1) * 128],
                       yT[:, dc * T_SEQ + g * C: dc * T_SEQ + (g + 1) * C],
                       start=(dc == 0), stop=(dc == NCH - 1),
                       inc=(sP if dc == NCH - 1 else None))
                up = p.counts["sP"]
                if hh == hs[0]:
                    p.wait("gpsimd", *osl)
                p.wait("gpsimd", sP, up)
                sli = Oslc[:, (sl * H + hh) * C:(sl * H + hh + 1) * C]
                pool_op(lambda e, a=sli, hh=hh: e.tensor_mul(
                    o2b[:, (hh % 2) * C:(hh % 2 + 1) * C], a, a), inc=sO)
                pend.append((hh, p.counts["sO"]))
                p.wait("scalar", sP, up)
                if hh == hs[0]:
                    p.wait("scalar", sO, gmult_done[g - 2])
                    p.wait("scalar", *osl)
                ub = prm[:, PC_QKVU_B + 24 + hh:PC_QKVU_B + 24 + hh + 1]
                act(ubuf[:, (sl * H + hh) * C:(sl * H + hh + 1) * C], bank[:],
                    AF.Identity, bias=ub, inc=sS)
                u_kr[(g, hh)] = p.counts["sS"]
                if len(pend) >= 2:
                    flush_row()
            while pend:
                flush_row()
            if hs[-1] == H - 1:
                u_msp[g] = p.counts["sP"]
                u_sq_done[g] = p.counts["sO"]

        def srms_chain(g):
            p.wait("scalar", sP, u_msp[g])
            act(srow[0:1, :], psMS[0:1, :], AF.Sqrt, scale=1.0 / D, bias=EPS_SRMS, inc=sS)
            srow_done[g] = p.counts["sS"]
            p.wait("vector", sS, srow_done[g])
            def _rcp(v):
                with nc.allow_low_precision("rs row cast to bf16 for matmul broadcast"):
                    return v.reciprocal(rsrowb, srow[0:1, :])
            vop(_rcp, inc=sV)
            rsc = p.counts["sV"]
            p.wait("tensor", sV, rsc)
            p.wait("tensor", sS, srow_done[g])
            mm(psMS[:, :], ones_row, rsrowb, start=True, stop=True, inc=sP)
            rsbp = p.counts["sP"]
            p.wait("vector", sP, rsbp)
            p.wait("vector", sO, gmult_done[g - 1])
            vop(lambda v: v.tensor_copy(rsb[:], psMS[:, :]), inc=sV)
            rsb_map[g] = p.counts["sV"]
            rsb_ev[g] = rsb_map[g]

        def gmult(g):
            sl = g % 2
            p.wait("gpsimd", sV, rsb_map[g])
            p.wait("gpsimd", sS, u_kr[(g, H - 1)])
            p.wait("gpsimd", *osl_map[g])
            for hh in range(H):
                sli = Oslc[:, (sl * H + hh) * C:(sl * H + hh + 1) * C]
                pool_op(lambda e, a=sli: e.tensor_mul(a, a, rsb[:]),
                        inc=(sO if hh == H - 1 else None))
            for hh in range(H):
                sli = Oslc[:, (sl * H + hh) * C:(sl * H + hh + 1) * C]
                pool_op(lambda e, a=sli, hh=hh: e.tensor_mul(
                    a, a, ubuf[:, (sl * H + hh) * C:(sl * H + hh + 1) * C]),
                    inc=(sO if hh == H - 1 else None))
            gmult_done[g] = p.counts["sO"]

        def zmm(g, tt):
            t = g * 4 + tt
            b = t % 2
            xsl = xbuf[:, b * D:(b + 1) * D]
            p.wait("sync", sV, last_x_cons[b])
            p.dma("sync", xsl, x_in[t * 128:(t + 1) * 128, :], sX[b])
            xl = (sX[b], p.counts[sX[b]])
            pr = pairs[tt % 2]
            base = (g % 2) * H * C
            p.wait("tensor", sO, gmult_done[g])
            p.wait("tensor", sS, zl_done[t - 2])
            for dc in range(NCH):
                for cg in range(2):
                    mm(pr[cg][:], Oslc[:, base + dc * C + tt * 128: base + dc * C + (tt + 1) * 128],
                       WoT[:, dc * D + cg * C: dc * D + (cg + 1) * C],
                       start=(dc == 0), stop=False)
            for cg in range(2):
                mm(pr[cg][:], ones_row, rowsbf[0:1, D + cg * C: D + (cg + 1) * C],
                   start=False, stop=True, inc=(sP if cg == 1 else None))
            zmm_zp[t] = p.counts["sP"]
            zmm_last[g] = p.counts["sP"]
            return xl

        def zchain(g, tt, xl):
            t = g * 4 + tt
            sl2 = tt % 2
            pr = pairs[sl2]
            st = stat6[:, sl2 * 12:(sl2 + 1) * 12]
            mvs = mv[:, sl2 * 2:(sl2 + 1) * 2]
            sq = sqv[:, sl2:sl2 + 1]
            rs = rstd[:, sl2:sl2 + 1]
            bb = b2[:, sl2:sl2 + 1]
            p.wait("vector", sP, zmm_zp[t])
            vop(lambda v, pr=pr, st=st: v.bn_stats(st[:, 0:6], pr[0][:]))
            vop(lambda v, pr=pr, st=st: v.bn_stats(st[:, 6:12], pr[1][:]), inc=sV)
            p.wait("vector", sV)
            vop(lambda v, st=st, mvs=mvs: v.bn_aggr(mvs, st), inc=sV)
            ag = p.counts["sV"]
            p.wait("scalar", sV, ag)
            p.wait("scalar", sV, y1recip[t - 2])
            act(sq, mvs[:, 1:2], AF.Sqrt, bias=EPS_LN, inc=sS)
            sqd = p.counts["sS"]
            p.wait("vector", sS, sqd)
            vop(lambda v, rs=rs, sq=sq: v.reciprocal(rs, sq), inc=sV)
            p.wait("vector", sV)
            vop(lambda v, bb=bb, mvs=mvs, rs=rs: v.tensor_mul(bb, mvs[:, 0:1], rs), inc=sV)
            p.wait("vector", sV)
            vop(lambda v, bb=bb: v.tensor_scalar(bb, bb, -1.0, None, ALU.mult), inc=sV)
            b2d = p.counts["sV"]
            p.wait("scalar", sV, b2d)
            act(ybuf[:, sl2 * D: sl2 * D + C], pr[0][:], AF.Identity, scale=rs, bias=bb)
            act(ybuf[:, sl2 * D + C:(sl2 + 1) * D], pr[1][:], AF.Identity, scale=rs, bias=bb, inc=sS)
            zl_done[t] = p.counts["sS"]
            b = t % 2
            xsl = xbuf[:, b * D:(b + 1) * D]
            p.wait("vector", sS, zl_done[t])
            p.wait("vector", *xl)
            p.wait("vector", sSTx[sl2], x1st[sl2])
            vop(lambda v, a=xsl, sl2=sl2: v.tensor_add(
                x1b[:, sl2 * D:(sl2 + 1) * D], ybuf[:, sl2 * D:(sl2 + 1) * D], a), inc=sV)
            x1v = p.counts["sV"]
            last_x_cons[b] = x1v
            p.wait("sync", sV, x1v)
            p.dma("sync", x1_dram[t * 128:(t + 1) * 128, :], x1b[:, sl2 * D:(sl2 + 1) * D], sSTx[sl2])
            if _dbg("x1"):
                p.dma("sync", dbg["d_x1"][t * 128:(t + 1) * 128, :],
                      x1b[:, sl2 * D:(sl2 + 1) * D], sSTx[sl2])
            x1st[sl2] = p.counts[sSTx[sl2]]

        def y1chain(g, tt):
            t = g * 4 + tt
            sl2 = tt % 2
            st = stat6[:, sl2 * 12:(sl2 + 1) * 12]
            mvs = mv[:, sl2 * 2:(sl2 + 1) * 2]
            sq = sqv[:, sl2:sl2 + 1]
            rs = rstd[:, sl2:sl2 + 1]
            xs = x1b[:, sl2 * D:(sl2 + 1) * D]
            p.wait("vector", sV)
            vop(lambda v, xs=xs, st=st: v.bn_stats(st[:, 0:6], xs[:, 0:512]))
            vop(lambda v, xs=xs, st=st: v.bn_stats(st[:, 6:12], xs[:, 512:1024]), inc=sV)
            p.wait("vector", sV)
            vop(lambda v, st=st, mvs=mvs: v.bn_aggr(mvs, st), inc=sV)
            ag = p.counts["sV"]
            p.wait("scalar", sV, ag)
            act(sq, mvs[:, 1:2], AF.Sqrt, bias=EPS_LN, inc=sS)
            sqd = p.counts["sS"]
            p.wait("vector", sS, sqd)
            vop(lambda v, rs=rs, sq=sq: v.reciprocal(rs, sq), inc=sV)
            y1recip[t] = p.counts["sV"]
            p.wait("vector", sV)
            p.wait("vector", sP, ytr_d[t - 2])
            vop(lambda v, xs=xs, mvs=mvs, rs=rs, sl2=sl2: v.tensor_scalar(
                y1b[:, sl2 * D:(sl2 + 1) * D], xs, mvs[:, 0:1], rs,
                ALU.subtract, ALU.mult), inc=sV)
            y1tv[t] = p.counts["sV"]

        def ytr(g, tt):
            t = g * 4 + tt
            sl2 = tt % 2
            p.wait("tensor", sV, y1tv[t])
            p.wait("tensor", sV, yev_d[t - 1])
            for j in range(NCH):
                tr(psT[:, j * 128:(j + 1) * 128],
                   y1b[:, sl2 * D + j * 128: sl2 * D + (j + 1) * 128],
                   inc=(sP if j == NCH - 1 else None))
            ytr_d[t] = p.counts["sP"]

        def yev(g, tt):
            t = g * 4 + tt
            p.wait("vector", sP, ytr_d[t])
            dst = yT[:].rearrange("p (c n) -> p c n", c=NCH)[:, :, t * 128:(t + 1) * 128]
            vop(lambda v, d=dst: v.tensor_copy(
                d, psT[:].rearrange("p (c n) -> p c n", c=NCH)), inc=sV)
            yev_d[t] = p.counts["sV"]

        oslc_load(0)
        u_compute(0, list(range(H)))
        srms_chain(0)
        gmult(0)
        for g in range(NG):
            if g == NG - 1:
                p.wait("sync", sP, u_msp[NG - 1])
                p.wait("sync", sGc4)
                p.dma("sync", WrT[:].rearrange("p (c n) -> p c n", c=NCH),
                      wcast[2 * D:3 * D, :].rearrange("(c p) n -> p c n", p=128), sGw3)
            if g + 1 < NG:
                oslc_load(g + 1)
            xl0 = zmm(g, 0)
            xl1 = zmm(g, 1)
            zchain(g, 0, xl0)
            if g == 2:
                xrk_subs(0, yev_d[3])
            if g + 1 < NG:
                u_compute(g + 1, list(range(H)))
                srms_chain(g + 1)
                gmult(g + 1)
            if g > 0:
                ytr(g - 1, 3)
                yev(g - 1, 3)
            y1chain(g, 0)
            if g == 7:
                p.wait("vector", sO, gmult_done[6])
                xrk_stt(0, [0, 1])
            xl2 = zmm(g, 2)
            ytr(g, 0)
            yev(g, 0)
            zchain(g, 1, xl1)
            y1chain(g, 1)
            if g == 7:
                xrk_stt(0, [2, 3])
            xl3 = zmm(g, 3)
            ytr(g, 1)
            yev(g, 1)
            zchain(g, 2, xl2)
            y1chain(g, 2)
            if g == 7:
                xrk_stt(0, [4, 5])
            ytr(g, 2)
            yev(g, 2)
            zchain(g, 3, xl3)
            y1chain(g, 3)
            if g == 7:
                xrk_stt(0, [6, 7])
            if g + 1 == NG:
                ytr(g, 3)
                yev(g, 3)

        y1T_done = p.counts["sV"]
        if _dbg("y1T"):
            p.wait("sync", sV, y1T_done)
            p.dma("sync", dbg["d_y1T"], yT[:], sST)

        p.wait("sync", sP)
        p.wait("sync", sV)
        p.wait("sync", sS)
        p.dma("sync", WkT[:].rearrange("p (c n) -> p c n", c=NCH), wcast[3 * D:4 * D, :].rearrange("(c p) n -> p c n", p=128), sGw4)
        p.dma("sync", WvT[:].rearrange("p (c n) -> p c n", c=NCH), wcast[4 * D:5 * D, :].rearrange("(c p) n -> p c n", p=128), sGw5)

        p.wait("tensor", sGw3, 16)
        p.wait("tensor", sS)
        p.wait("tensor", sV)
        f_done = {}
        sig_done = {}
        out_store = {0: 0, 1: 0}
        out_store_st = {0: 0, 1: 0}
        f_done[-1] = 0; f_done[-2] = 0

        for g in range(NG):
            sb = (g % 2) * NCH * C

            def r_mms(tt):
                t = g * 4 + tt
                pr = pairs[t % 2]
                p.wait("tensor", sV, xrk_done[g])
                p.wait("tensor", sV, f_done[t - 2])
                for dc in range(NCH):
                    for cg in range(2):
                        mm(pr[cg][:], xrb[:, sb + dc * C + tt * 128: sb + dc * C + (tt + 1) * 128],
                           WrT[:, dc * D + cg * C: dc * D + (cg + 1) * C],
                           start=(dc == 0), stop=(dc == NCH - 1),
                           inc=(sP if (cg == 1 and dc == NCH - 1) else None))
                return p.counts["sP"]

            def sig(tt, rp):
                t = g * 4 + tt
                pr = pairs[t % 2]
                p.wait("scalar", sP, rp)
                p.wait("scalar", sV, f_done[t - 2])
                act(rbuf[:, (t % 2) * D:(t % 2) * D + C], pr[0][:], AF.Sigmoid)
                act(rbuf[:, (t % 2) * D + C:(t % 2 + 1) * D], pr[1][:], AF.Sigmoid, inc=sS)
                sig_done[t] = p.counts["sS"]

            def h_mms(tt):
                t = g * 4 + tt
                pr = pairs[t % 2]
                if g == 0 and tt == 0:
                    p.wait("tensor", sGw5, 16)
                p.wait("tensor", sS, sig_done[t])
                p.wait("tensor", sO, sq_done)
                for dc in range(NCH):
                    for cg in range(2):
                        mm(pr[cg][:], kact[:, dc * C + tt * 128: dc * C + (tt + 1) * 128],
                           WvT[:, dc * D + cg * C: dc * D + (cg + 1) * C],
                           start=(dc == 0), stop=(dc == NCH - 1),
                           inc=(sP if (cg == 1 and dc == NCH - 1) else None))
                return p.counts["sP"]

            def f_ops(tt, hp):
                t = g * 4 + tt
                sl = t % 2
                pr = pairs[sl]
                fs = fsb[:, sl * D:(sl + 1) * D]
                p.wait("vector", sP, hp)
                p.wait("vector", sS, sig_done[t])
                vop(lambda v, fs=fs, pr=pr, sl=sl: v.tensor_mul(
                    fs[:, 0:C], pr[0][:], rbuf[:, sl * D: sl * D + C]))
                vop(lambda v, fs=fs, pr=pr, sl=sl: v.tensor_mul(
                    fs[:, C:D], pr[1][:], rbuf[:, sl * D + C:(sl + 1) * D]), inc=sV)
                f_done[t] = p.counts["sV"]
                p.wait("vector", sV, f_done[t])
                st = stat6[:, sl * 12:(sl + 1) * 12]
                vop(lambda v, fs=fs, st=st: v.bn_stats(st[:, 0:6], fs[:, 0:512]))
                vop(lambda v, fs=fs, st=st: v.bn_stats(st[:, 6:12], fs[:, 512:1024]), inc=sV)
                p.wait("vector", sV)
                vop(lambda v, st=st, sl=sl: v.bn_aggr(mv[:, sl * 2:(sl + 1) * 2], st), inc=sV)
                return p.counts["sV"]

            def sqrt_op(tt, aggr):
                t = g * 4 + tt
                sl = t % 2
                p.wait("scalar", sV, aggr)
                p.wait("scalar", sV, tail_done[t - 2])
                act(sqv[:, sl:sl + 1], mv[:, sl * 2 + 1:sl * 2 + 2], AF.Sqrt,
                    bias=EPS_LN, inc=sS)
                return p.counts["sS"]

            def tail(tt, sq, x1ct):
                t = g * 4 + tt
                sl = t % 2
                fs = fsb[:, sl * D:(sl + 1) * D]
                p.wait("vector", sS, sq)
                vop(lambda v, sl=sl: v.reciprocal(rstd[:, sl:sl + 1], sqv[:, sl:sl + 1]), inc=sV)
                p.wait("vector", sV)
                vop(lambda v, fs=fs, sl=sl: v.tensor_scalar(
                    ybuf[:, 0:D], fs, mv[:, sl * 2:sl * 2 + 1], rstd[:, sl:sl + 1],
                    ALU.subtract, ALU.mult), inc=sV)
                p.wait("vector", sV)
                p.wait("vector", *x1ct)
                p.wait("vector", sSTu[sl], out_store_st[sl])
                vop(lambda v, sl=sl: v.tensor_add(
                    outb[:, sl * D:(sl + 1) * D], ybuf[:, 0:D], x1l[:, sl * D:(sl + 1) * D]), inc=sV)
                tail_done[t] = p.counts["sV"]
                out_store[sl] = tail_done[t]
                p.wait("sync", sV, tail_done[t])
                p.dma("sync", out_hbm[t * 128:(t + 1) * 128, :], outb[:, sl * D:(sl + 1) * D], sSTu[sl])
                out_store_st[sl] = p.counts[sSTu[sl]]

            if g == 0:
                tail_done = {-2: 0, -1: 0}
            x1c = {}

            def x1_load(tt):
                t = g * 4 + tt
                p.wait("sync", sV, out_store[t % 2])
                if g == 0:
                    p.wait("sync", sSTx[0])
                    p.wait("sync", sSTx[1])
                p.dma("sync", x1l[:, (t % 2) * D:(t % 2 + 1) * D],
                      x1_dram[t * 128:(t + 1) * 128, :], sX1L[t % 2])
                x1c[tt] = (sX1L[t % 2], p.counts[sX1L[t % 2]])

            x1_load(0)
            x1_load(1)
            rp0 = r_mms(0)
            rp1 = r_mms(1)
            sig(0, rp0)
            sig(1, rp1)
            if g + 1 < NG:
                xrk_subs(g + 1, y1T_done)
            kr = {}
            sqk = {-2: 0, -1: 0}
            p.wait("tensor", sV, xrk_done[g])
            if g == 0:
                p.wait("tensor", sGw4, 16)
            for kc in range(NCH):
                bank = psKt if kc % 2 == 0 else psKo
                p.wait("tensor", sS, kr.get(kc - 2, 0))
                for dc in range(NCH):
                    mm(bank[:], WkT[:, dc * D + kc * 128: dc * D + (kc + 1) * 128],
                       xkb[:, sb + dc * C: sb + (dc + 1) * C],
                       start=(dc == 0), stop=(dc == NCH - 1),
                       inc=(sP if dc == NCH - 1 else None))
                kp = p.counts["sP"]
                p.wait("scalar", sP, kp)
                p.wait("scalar", sO, sqk[kc - 2])
                act(o2b[:, (kc % 2) * C:(kc % 2 + 1) * C], bank[:], AF.Relu, inc=sS)
                kr[kc] = p.counts["sS"]
                p.wait("gpsimd", sS, kr[kc])
                pool_op(lambda e, kc=kc: e.tensor_mul(
                    kact[:, kc * C:(kc + 1) * C],
                    o2b[:, (kc % 2) * C:(kc % 2 + 1) * C],
                    o2b[:, (kc % 2) * C:(kc % 2 + 1) * C]), inc=sO)
                sqk[kc] = p.counts["sO"]
            sq_done = p.counts["sO"]

            hp0 = h_mms(0)
            hp1 = h_mms(1)
            a0 = f_ops(0, hp0)
            if g + 1 < NG:
                xrk_stt(g + 1, [0, 1])
            rp2 = r_mms(2)
            s0 = sqrt_op(0, a0)
            sig(2, rp2)
            hp2 = h_mms(2)
            a1 = f_ops(1, hp1)
            if g + 1 < NG:
                xrk_stt(g + 1, [2, 3])
            rp3 = r_mms(3)
            s1 = sqrt_op(1, a1)
            sig(3, rp3)
            tail(0, s0, x1c[0])
            x1_load(2)
            hp3 = h_mms(3)
            a2 = f_ops(2, hp2)
            if g + 1 < NG:
                xrk_stt(g + 1, [4, 5])
            s2 = sqrt_op(2, a2)
            tail(1, s1, x1c[1])
            x1_load(3)
            a3 = f_ops(3, hp3)
            if g + 1 < NG:
                xrk_stt(g + 1, [6, 7])
            s3 = sqrt_op(3, a3)
            tail(2, s2, x1c[2])
            prev_rh_P[g] = p.counts["sP"]
            tail(3, s3, x1c[3])

        p.wait("sync", sSTu[0])
        p.wait("sync", sSTu[1])
        p.wait("sync", sSTx[0])
        p.wait("sync", sSTx[1])
        p.wait("sync", sSTo[0])
        p.wait("sync", sSTo[1])
        if p.counts["sST"] > 0:
            p.wait("sync", sST, p.counts["sST"])

        with nc.Block() as block:
            p.replay(block)

    return nc, sorted(dbg)


_CACHED = {}


def _get_program():
    key = tuple(sorted(DEBUG_TAPS))
    if key not in _CACHED:
        _CACHED[key] = build_program()
    return _CACHED[key]


def make_in_maps(L0_attn_x, L0_attn_slope_rate, L0_attn_qkvu_w, L0_attn_qkvu_b,
                 L0_attn_out_w, L0_attn_out_b, L0_attn_theta,
                 L0_ffn_W_r, L0_ffn_W_k, L0_ffn_W_v, L0_ffn_mu_r, L0_ffn_mu_k):
    f32 = np.float32
    wqkvuT = np.ascontiguousarray(np.asarray(L0_attn_qkvu_w, f32).T)
    woT = np.ascontiguousarray(np.asarray(L0_attn_out_w, f32).T)
    wrT = np.ascontiguousarray(np.asarray(L0_ffn_W_r, f32).T)
    wkT = np.ascontiguousarray(np.asarray(L0_ffn_W_k, f32).T)
    wvT = np.ascontiguousarray(np.asarray(L0_ffn_W_v, f32).T)

    prm = np.zeros((128, PCOLS), f32)
    prm[:, PC_QKVU_B:PC_QKVU_B + 32] = np.asarray(L0_attn_qkvu_b, f32).reshape(32, 128).T
    prm[:, PC_MU_R:PC_MU_R + 8] = np.asarray(L0_ffn_mu_r, f32).reshape(8, 128).T
    prm[:, PC_MU_K:PC_MU_K + 8] = np.asarray(L0_ffn_mu_k, f32).reshape(8, 128).T
    prm[:, PC_THETA:PC_THETA + 8] = np.asarray(L0_attn_theta, f32).reshape(8, 128).T
    prm[:, PC_SLOPE:PC_SLOPE + 8] = np.broadcast_to(
        np.asarray(L0_attn_slope_rate, f32).reshape(1, 8), (128, 8))

    rows = np.zeros((1, ROWS_N), f32)
    rows[0, 0:D] = np.asarray(L0_attn_qkvu_b, f32)[2 * D:3 * D]
    rows[0, D:2 * D] = np.asarray(L0_attn_out_b, f32)
    rows[0, 2 * D:2 * D + 128] = 1.0

    consts = np.broadcast_to(
        np.asarray(CONST_VALS, f32)[None, :], (128, len(CONST_VALS))).copy()
    cones = np.zeros((128, 2), f32)
    cones[:, 0] = 1.0
    ident = np.eye(128, dtype=ml_dtypes.bfloat16)
    triu = np.triu(np.ones((C, C))).astype(ml_dtypes.bfloat16)

    x = np.asarray(L0_attn_x, f32)
    in_maps = []
    for b in range(x.shape[0]):
        in_maps.append(dict(
            x_in=np.ascontiguousarray(x[b]),
            wqkvuT=wqkvuT, woT=woT, wrT=wrT, wkT=wkT, wvT=wvT,
            params=prm, rows=rows, consts=consts, cones=cones,
            ident=ident, triu=triu,
        ))
    return in_maps


def kernel(**inputs):
    nc, _ = _get_program()
    in_maps = make_in_maps(**inputs)
    res = run_bass_kernel_spmd(nc, in_maps, core_ids=list(range(len(in_maps))))
    return np.stack([r["out"] for r in res.results], axis=0)


# version 39 (final)
# speedup vs baseline: 1.0396x; 1.0396x over previous
import numpy as np
import ml_dtypes
from contextlib import ExitStack

import concourse.bass as bass
import concourse.mybir as mybir
from concourse.bass_utils import run_bass_kernel_spmd

F32 = mybir.dt.float32
BF16 = mybir.dt.bfloat16
AF = mybir.ActivationFunctionType
ALU = mybir.AluOpType

PI = float(np.pi)
TWO_PI = float(2 * np.pi)
TWO_PI_LO = float(np.nextafter(np.float32(TWO_PI), np.float32(0.0)))

T_SEQ = 4096
D = 1024
H = 8
HD = 128
NCH = 8
C = 512
NG = T_SEQ // C
NT = T_SEQ // 128
LEV = 12
EPS_LN = 1e-5
EPS_SRMS = 1e-6

PC_QKVU_B = 0
PC_MU_R = 32
PC_MU_K = 40
PC_THETA = 48
PC_SLOPE = 56
PCOLS = 64
ROWS_N = 2 * D + 128

CONST_VALS = [-PI, -PI / 2, EPS_LN, EPS_SRMS, 1.0 / D, PI / 2, -1.0, 2.0, -TWO_PI, 0.5, 1.5 * PI]

DEBUG_TAPS = []
TRI_SCALAR_EVAC = False


def _dbg(n):
    return n in DEBUG_TAPS


class Prog:

    ENGS = ("sync", "tensor", "vector", "scalar", "gpsimd")

    def __init__(self, nc):
        self.nc = nc
        self.ops = {e: [] for e in self.ENGS}
        self.sems = {}
        self.counts = {}

    def sem(self, name):
        self.sems[name] = self.nc.alloc_semaphore(name)
        self.counts[name] = 0
        return name

    def wait(self, eng, sem, val=None):
        if val is None:
            val = self.counts[sem]
        assert val <= self.counts[sem], (eng, sem, val, self.counts[sem])
        if val <= 0:
            return
        h = self.sems[sem]
        self.ops[eng].append(lambda e, h=h, v=val: e.wait_ge(h, v))

    def emit(self, eng, fn, inc=None, inc_n=1):
        if inc is not None:
            h = self.sems[inc]
            self.ops[eng].append(lambda e, fn=fn, h=h, n=inc_n: fn(e).then_inc(h, n))
            self.counts[inc] += inc_n
            return self.counts[inc]
        self.ops[eng].append(fn)
        return None

    def dma(self, eng, out, in_, sem):
        return self.emit(eng, lambda e, o=out, i=in_: e.dma_start(out=o, in_=i),
                         inc=sem, inc_n=16)

    def replay(self, block):
        def mk(name):
            def run(eng):
                for fn in self.ops[name]:
                    fn(eng)
            return run
        block.sync(mk("sync"))
        block.tensor(mk("tensor"))
        block.vector(mk("vector"))
        block.scalar(mk("scalar"))
        block.gpsimd(mk("gpsimd"))


def build_program():
    nc = bass.Bass("TRN2", target_bir_lowering=False)

    x_in = nc.dram_tensor("x_in", [T_SEQ, D], F32, kind="ExternalInput").ap()
    wqkvuT = nc.dram_tensor("wqkvuT", [D, 4 * D], F32, kind="ExternalInput").ap()
    woT_in = nc.dram_tensor("woT", [D, D], F32, kind="ExternalInput").ap()
    wrT_in = nc.dram_tensor("wrT", [D, D], F32, kind="ExternalInput").ap()
    wkT_in = nc.dram_tensor("wkT", [D, D], F32, kind="ExternalInput").ap()
    wvT_in = nc.dram_tensor("wvT", [D, D], F32, kind="ExternalInput").ap()
    params = nc.dram_tensor("params", [128, PCOLS], F32, kind="ExternalInput").ap()
    rows_in = nc.dram_tensor("rows", [1, ROWS_N], F32, kind="ExternalInput").ap()
    consts_in = nc.dram_tensor("consts", [128, len(CONST_VALS)], F32, kind="ExternalInput").ap()
    cones_in = nc.dram_tensor("cones", [128, 2], F32, kind="ExternalInput").ap()
    ident_in = nc.dram_tensor("ident", [128, 128], BF16, kind="ExternalInput").ap()
    triu_in = nc.dram_tensor("triu", [C, C], BF16, kind="ExternalInput").ap()

    out_hbm = nc.dram_tensor("out", [T_SEQ, D], F32, kind="ExternalOutput").ap()
    o_spill = nc.dram_tensor("o_spill", [H * 128, T_SEQ], BF16, kind="Internal").ap()
    x1_dram = nc.dram_tensor("x1_dram", [T_SEQ, D], BF16, kind="Internal").ap()
    wcast = nc.dram_tensor("wcast", [5 * D, D], BF16, kind="Internal").ap()

    dbg = {}
    if _dbg("yT"):
        dbg["d_yT"] = nc.dram_tensor("d_yT", [128, NCH * T_SEQ], BF16, kind="ExternalOutput").ap()
    if _dbg("tabs"):
        dbg["d_tabS"] = nc.dram_tensor("d_tabS", [128, T_SEQ], BF16, kind="ExternalOutput").ap()
        dbg["d_tabC"] = nc.dram_tensor("d_tabC", [128, T_SEQ], BF16, kind="ExternalOutput").ap()
    if _dbg("kv"):
        dbg["d_Kc"] = nc.dram_tensor("d_Kc", [128, T_SEQ], BF16, kind="ExternalOutput").ap()
        dbg["d_Ks"] = nc.dram_tensor("d_Ks", [128, T_SEQ], BF16, kind="ExternalOutput").ap()
        dbg["d_Vt"] = nc.dram_tensor("d_Vt", [128, T_SEQ], BF16, kind="ExternalOutput").ap()
    if _dbg("oh"):
        dbg["d_Oh"] = nc.dram_tensor("d_Oh", [128, T_SEQ], BF16, kind="ExternalOutput").ap()
    if _dbg("att"):
        dbg["d_KVt"] = nc.dram_tensor("d_KVt", [128, 256], BF16, kind="ExternalOutput").ap()
        dbg["d_Sarr"] = nc.dram_tensor("d_Sarr", [128, NCH * 256], BF16, kind="ExternalOutput").ap()
        dbg["d_Qt"] = nc.dram_tensor("d_Qt", [128, 1024], BF16, kind="ExternalOutput").ap()
        dbg["d_Am"] = nc.dram_tensor("d_Am", [128, 4 * C], BF16, kind="ExternalOutput").ap()
        dbg["d_mask"] = nc.dram_tensor("d_mask", [128, 4 * C], BF16, kind="ExternalOutput").ap()
    if _dbg("x1"):
        dbg["d_x1"] = nc.dram_tensor("d_x1", [T_SEQ, D], BF16, kind="ExternalOutput").ap()
    if _dbg("y1T"):
        dbg["d_y1T"] = nc.dram_tensor("d_y1T", [128, NCH * T_SEQ], BF16, kind="ExternalOutput").ap()

    if True:
        e = lambda t: t

        yT = e(nc.alloc_sbuf_tensor("yT", [128, NCH * T_SEQ], BF16))
        xbuf = e(nc.alloc_sbuf_tensor("xbuf", [128, 2 * D], F32))
        ybuf = e(nc.alloc_sbuf_tensor("ybuf", [128, 2 * D], BF16))
        x1b = e(nc.alloc_sbuf_tensor("x1b", [128, 2 * D], BF16))
        y1b = e(nc.alloc_sbuf_tensor("y1b", [128, 2 * D], BF16))
        x1l = e(nc.alloc_sbuf_tensor("x1l", [128, 2 * D], BF16))
        outb = e(nc.alloc_sbuf_tensor("outb", [128, 2 * D], F32))
        o2b = e(nc.alloc_sbuf_tensor("o2b", [128, 2 * C], BF16))
        stat6 = e(nc.alloc_sbuf_tensor("stat6", [128, 24], F32))
        mv = e(nc.alloc_sbuf_tensor("mv", [128, 4], F32))
        rstd = e(nc.alloc_sbuf_tensor("rstd", [128, 2], F32))
        sqv = e(nc.alloc_sbuf_tensor("sqv", [128, 2], F32))
        b2 = e(nc.alloc_sbuf_tensor("b2", [128, 2], F32))
        prm = e(nc.alloc_sbuf_tensor("prm", [128, PCOLS], F32))
        es1 = e(nc.alloc_sbuf_tensor("es1", [128, H], F32))
        cst = e(nc.alloc_sbuf_tensor("cst", [128, len(CONST_VALS)], F32))
        onescol = e(nc.alloc_sbuf_tensor("onescol", [128, 2], BF16))
        rowsbf = e(nc.alloc_sbuf_tensor("rowsbf", [1, ROWS_N], BF16))
        ident = e(nc.alloc_sbuf_tensor("ident_sb", [128, 128], BF16))
        w_all = e(nc.alloc_sbuf_tensor("w_all", [128, (LEV + 1) * H], F32))
        thr_all = e(nc.alloc_sbuf_tensor("thr_all", [128, LEV * H], F32))
        srow = e(nc.alloc_sbuf_tensor("srow", [1, C], BF16))
        rsb = e(nc.alloc_sbuf_tensor("rsb", [128, C], BF16))
        rsrowb = rsb[0:1, :]

        for i, v in enumerate(CONST_VALS):
            nc.const_aps.register(F32, v, cst[:, i:i + 1])

        A1 = nc.auto_arena()
        treeT = A1.sbuf_tensor("treeT", [128, T_SEQ], F32)
        tabS = A1.sbuf_tensor("tabS", [128, T_SEQ], BF16)
        tabC = A1.sbuf_tensor("tabC", [128, T_SEQ], BF16)
        Kc = A1.sbuf_tensor("Kc", [128, T_SEQ], BF16)
        Ks = A1.sbuf_tensor("Ks", [128, T_SEQ], BF16)
        Vtok = A1.sbuf_tensor("Vtok", [128, T_SEQ], BF16)
        Ktok = A1.sbuf_tensor("Ktok", [128, 2048], BF16)
        Qt = A1.sbuf_tensor("Qt", [128, 2048], BF16)
        psil = A1.sbuf_tensor("psil", [128, 2 * C], BF16)
        Am = A1.sbuf_tensor("Am", [128, 8 * C], BF16)
        maskh = A1.sbuf_tensor("maskh", [128, 128], BF16)
        KVall = A1.sbuf_tensor("KVall", [128, NCH * 256], BF16)
        KVtot = A1.sbuf_tensor("KVtot", [128, 256], BF16)
        Sfp = A1.sbuf_tensor("Sfp", [128, 256], F32)
        Sarr = A1.sbuf_tensor("Sarr", [128, NCH * 256], BF16)
        Oh = A1.sbuf_tensor("Oh", [128, 2 * C], BF16)
        wq_h = A1.sbuf_tensor("wq_h", [128, 2 * 1024], BF16)
        wk_h = A1.sbuf_tensor("wk_h", [128, 2 * 1024], BF16)
        wv_h = A1.sbuf_tensor("wv_h", [128, 2 * 1024], BF16)

        A2 = nc.auto_arena()
        WuT = A2.sbuf_tensor("WuT", [128, NCH * D], BF16)
        WoT = A2.sbuf_tensor("WoT", [128, NCH * D], BF16)
        Oslc = A2.sbuf_tensor("Oslc", [128, 2 * H * C], BF16)
        ubuf = A2.sbuf_tensor("ubuf", [128, 2 * H * C], BF16)

        A3 = nc.auto_arena()
        WrT = A3.sbuf_tensor("WrT", [128, NCH * D], BF16)
        WkT = A3.sbuf_tensor("WkT", [128, NCH * D], BF16)
        WvT = A3.sbuf_tensor("WvT", [128, NCH * D], BF16)
        xrb = A3.sbuf_tensor("xrb", [128, 2 * NCH * C], BF16)
        xkb = A3.sbuf_tensor("xkb", [128, 2 * NCH * C], BF16)
        kact = A3.sbuf_tensor("kact", [128, NCH * C], BF16)
        rbuf = A3.sbuf_tensor("rbuf", [128, 2 * D], BF16)
        fsb = A3.sbuf_tensor("fsb", [128, 2 * D], F32)

        base0 = nc.sbuf_base
        A1.allocate()
        top1 = nc.sbuf_base
        nc.sbuf_base = base0
        A2.allocate()
        top2 = nc.sbuf_base
        nc.sbuf_base = base0
        A3.allocate()
        top3 = nc.sbuf_base
        nc.sbuf_base = max(top1, top2, top3)

        psA = e(nc.alloc_psum_tensor("psA", [128, C], F32))
        psB = e(nc.alloc_psum_tensor("psB", [128, C], F32))
        psC = e(nc.alloc_psum_tensor("psC", [128, C], F32))
        psO = e(nc.alloc_psum_tensor("psO", [128, C], F32))
        psKt = e(nc.alloc_psum_tensor("psKt", [128, C], F32))
        psKo = e(nc.alloc_psum_tensor("psKo", [128, C], F32))
        psMS = e(nc.alloc_psum_tensor("psMS", [128, C], F32))
        psT = e(nc.alloc_psum_tensor("psT", [128, 1024], BF16))

        p = Prog(nc)
        sL = p.sem("sL"); sG = p.sem("sG"); sP = p.sem("sP"); sO2 = p.sem("sO2")
        sV = p.sem("sV"); sS = p.sem("sS"); sST = p.sem("sST")
        sX = [p.sem("sX0"), p.sem("sX1")]
        sOs = [p.sem("sOs0"), p.sem("sOs1")]
        sX1L = [p.sem("sX1L0"), p.sem("sX1L1")]
        sGp = p.sem("sGp")
        sGw = [p.sem("sGw0"), p.sem("sGw1")]
        sGm = p.sem("sGm")
        sGw2 = p.sem("sGw2")
        sGc3 = p.sem("sGc3")
        sGc4 = p.sem("sGc4")
        sGw3 = p.sem("sGw3"); sGw4 = p.sem("sGw4"); sGw5 = p.sem("sGw5")
        sSTo = [p.sem("sSTo0"), p.sem("sSTo1")]
        sSTx = [p.sem("sSTx0"), p.sem("sSTx1")]
        sSTu = [p.sem("sSTu0"), p.sem("sSTu1")]

        ones_row = rowsbf[0:1, 2 * D:2 * D + 128]

        def mm(out, lhsT, rhs, start, stop, inc=None):
            return p.emit("tensor", lambda t, o=out, l=lhsT, r=rhs, a=start, b=stop:
                          t.matmul(o, lhsT=l, rhs=r, start=a, stop=b), inc=inc)

        def tr(out, in_, inc=None):
            return p.emit("tensor", lambda t, o=out, i=in_:
                          t.transpose(o, in_=i, identity=ident[:]), inc=inc)

        def act(out, in_, func, bias=0.0, scale=1.0, accum=None, inc=None):
            return p.emit("scalar", lambda s, o=out, i=in_, f=func, b=bias, sc=scale, a=accum:
                          s.activation(o, i, f, bias=b, scale=sc, accum_out=a), inc=inc)

        def vop(fn, inc=None):
            return p.emit("vector", fn, inc=inc)

        def ln_stats(src0, src1, slot=0):
            st = stat6[:, slot * 12:(slot + 1) * 12]
            mvs = mv[:, slot * 2:(slot + 1) * 2]
            sq = sqv[:, slot:slot + 1]
            rs = rstd[:, slot:slot + 1]
            vop(lambda v, a=src0, st=st: v.bn_stats(st[:, 0:6], a))
            vop(lambda v, a=src1, st=st: v.bn_stats(st[:, 6:12], a), inc=sV)
            p.wait("vector", sV)
            vop(lambda v, st=st, mvs=mvs: v.bn_aggr(mvs, st), inc=sV)
            p.wait("scalar", sV)
            act(sq, mvs[:, 1:2], AF.Sqrt, bias=EPS_LN, inc=sS)
            p.wait("vector", sS)
            vop(lambda v, rs=rs, sq=sq: v.reciprocal(rs, sq), inc=sV)
            p.wait("vector", sV)

        p.dma("gpsimd", prm[:], params, sGp)
        p.dma("gpsimd", cst[:], consts_in, sGp)
        p.dma("gpsimd", onescol[:], cones_in, sGp)
        p.dma("gpsimd", rowsbf[:], rows_in, sGp)
        p.dma("gpsimd", ident[:], ident_in, sGp)

        def load_head_weights(h):
            for w, off in ((wq_h, 0), (wk_h, D), (wv_h, 2 * D)):
                src = wqkvuT[:, off + h * HD: off + (h + 1) * HD]
                src = src.rearrange("(c p) n -> p c n", p=128)
                dst = w[:, (h % 2) * 1024:(h % 2 + 1) * 1024].rearrange(
                    "p (c n) -> p c n", c=NCH)
                p.dma("gpsimd", dst, src, sGw[h % 2])
        load_head_weights(0)
        g_head_w = {0: p.counts["sGw0"]}

        p.wait("vector", sGp)
        vop(lambda v: v.tensor_copy(w_all[:, 0:H], prm[:, PC_THETA:PC_THETA + H]), inc=sV)
        p.wait("vector", sV)
        for k in range(LEV):
            wk = w_all[:, k * H:(k + 1) * H]
            wn = w_all[:, (k + 1) * H:(k + 2) * H]
            thr = thr_all[:, k * H:(k + 1) * H]
            vop(lambda v, a=thr, b=wk: v.tensor_scalar(a, b, -1.0, TWO_PI, ALU.mult, ALU.add))
            vop(lambda v, a=wn, b=wk: v.tensor_scalar(a, b, PI, -TWO_PI, ALU.is_ge, ALU.mult), inc=sV)
            p.wait("vector", sV)
            vop(lambda v, a=wn, b=wk: v.scalar_tensor_tensor(a, b, 2.0, a, ALU.mult, ALU.add), inc=sV)
            p.wait("vector", sV)
        wchain_done = p.counts["sV"]

        p.wait("scalar", sGp)
        act(es1[:], prm[:, PC_SLOPE:PC_SLOPE + H], AF.Exp, inc=sS)
        p.wait("vector", sS)
        vop(lambda v: v.tensor_scalar(es1[:], es1[:], -1.0, None, ALU.add), inc=sV)
        es1_ready = p.counts["sV"]

        last_x_cons = {0: 0, 1: 0}
        s1_ap = {-2: 0, -1: 0}
        s1_tr = {-2: 0, -1: 0}
        s1_ev = {-1: 0}
        s1_xl = {}
        s1_ag = {}
        s1_sq = {}

        def s1_load(t):
            b = t % 2
            p.wait("sync", sV, s1_ap[t - 2])
            p.dma("sync", xbuf[:, b * D:(b + 1) * D], x_in[t * 128:(t + 1) * 128, :], sX[b])
            s1_xl[t] = (sX[b], p.counts[sX[b]])

        def s1_stats(t):
            b = t % 2
            xsl = xbuf[:, b * D:(b + 1) * D]
            st = stat6[:, b * 12:(b + 1) * 12]
            p.wait("vector", *s1_xl[t])
            vop(lambda v, a=xsl, st=st: v.bn_stats(st[:, 0:6], a[:, 0:512]))
            vop(lambda v, a=xsl, st=st: v.bn_stats(st[:, 6:12], a[:, 512:1024]), inc=sV)
            p.wait("vector", sV)
            vop(lambda v, st=st, b=b: v.bn_aggr(mv[:, b * 2:(b + 1) * 2], st), inc=sV)
            s1_ag[t] = p.counts["sV"]

        def s1_sqrt(t):
            b = t % 2
            p.wait("scalar", sV, s1_ag[t])
            p.wait("scalar", sV, s1_ap[t - 2])
            act(sqv[:, b:b + 1], mv[:, b * 2 + 1:b * 2 + 2], AF.Sqrt, bias=EPS_LN, inc=sS)
            s1_sq[t] = p.counts["sS"]

        def s1_apply(t):
            b = t % 2
            xsl = xbuf[:, b * D:(b + 1) * D]
            p.wait("vector", sS, s1_sq[t])
            vop(lambda v, b=b: v.reciprocal(rstd[:, b:b + 1], sqv[:, b:b + 1]), inc=sV)
            p.wait("vector", sV)
            p.wait("vector", sP, s1_tr[t - 2])
            vop(lambda v, a=xsl, b=b: v.tensor_scalar(
                ybuf[:, b * D:(b + 1) * D], a, mv[:, b * 2:b * 2 + 1], rstd[:, b:b + 1],
                ALU.subtract, ALU.mult), inc=sV)
            s1_ap[t] = p.counts["sV"]
            last_x_cons[b] = s1_ap[t]

        def s1_tr_f(t):
            b = t % 2
            p.wait("tensor", sV, s1_ap[t])
            p.wait("tensor", sV, s1_ev[t - 1])
            for j in range(NCH):
                tr(psT[:, j * 128:(j + 1) * 128],
                   ybuf[:, b * D + j * 128: b * D + (j + 1) * 128],
                   inc=(sP if j == NCH - 1 else None))
            s1_tr[t] = p.counts["sP"]

        def s1_ev_f(t):
            p.wait("vector", sP, s1_tr[t])
            dst = yT[:].rearrange("p (c n) -> p c n", c=NCH)[:, :, t * 128:(t + 1) * 128]
            vop(lambda v, d=dst: v.tensor_copy(
                d, psT[:].rearrange("p (c n) -> p c n", c=NCH)), inc=sV)
            s1_ev[t] = p.counts["sV"]

        s1_load(0)
        s1_load(1)
        s1_stats(0)
        s1_sqrt(0)
        for t in range(NT):
            if t + 1 < NT:
                s1_stats(t + 1)
                s1_sqrt(t + 1)
            s1_apply(t)
            s1_tr_f(t)
            if t + 2 < NT:
                s1_load(t + 2)
            s1_ev_f(t)

        yT_done = p.counts["sV"]
        if _dbg("yT"):
            p.wait("sync", sV, yT_done)
            p.dma("sync", dbg["d_yT"], yT[:], sST)

        tabs_c = {}

        def tables_stage1(h):
            vop(lambda v: v.memset(treeT[:, 0:1], 0.0), inc=sV)
            p.wait("vector", sV)
            for k in range(10):
                wk = w_all[:, k * H + h:k * H + h + 1]
                thr = thr_all[:, k * H + h:k * H + h + 1]
                w_ = 1 << k
                vop(lambda v, w=w_, t=thr: v.tensor_scalar(
                    treeT[:, w:2 * w], treeT[:, 0:w], t, -TWO_PI, ALU.is_ge, ALU.mult), inc=sV)
                p.wait("vector", sV)
                vop(lambda v, w=w_, wp=wk: v.scalar_tensor_tensor(
                    treeT[:, w:2 * w], treeT[:, 0:w], wp, treeT[:, w:2 * w], ALU.add, ALU.add), inc=sV)
                p.wait("vector", sV)
            tables_region(h, 0, 1024, (0, 1))

        def tables_stage2(h):
            k = 10
            wk = w_all[:, k * H + h:k * H + h + 1]
            thr = thr_all[:, k * H + h:k * H + h + 1]
            w_ = 1 << k
            vop(lambda v, w=w_, t=thr: v.tensor_scalar(
                treeT[:, w:2 * w], treeT[:, 0:w], t, -TWO_PI, ALU.is_ge, ALU.mult), inc=sV)
            p.wait("vector", sV)
            vop(lambda v, w=w_, wp=wk: v.scalar_tensor_tensor(
                treeT[:, w:2 * w], treeT[:, 0:w], wp, treeT[:, w:2 * w], ALU.add, ALU.add), inc=sV)
            p.wait("vector", sV)
            tables_region(h, 1024, 2048, (2, 3))

        def tables_stage3(h):
            k = 11
            wk = w_all[:, k * H + h:k * H + h + 1]
            thr = thr_all[:, k * H + h:k * H + h + 1]
            w_ = 1 << k
            vop(lambda v, w=w_, t=thr: v.tensor_scalar(
                treeT[:, w:2 * w], treeT[:, 0:w], t, -TWO_PI, ALU.is_ge, ALU.mult), inc=sV)
            p.wait("vector", sV)
            vop(lambda v, w=w_, wp=wk: v.scalar_tensor_tensor(
                treeT[:, w:2 * w], treeT[:, 0:w], wp, treeT[:, w:2 * w], ALU.add, ALU.add), inc=sV)
            p.wait("vector", sV)
            tables_region(h, 2048, 4096, (4, 5, 6, 7))

        def tables_region(h, a, b, chunks):
            vop(lambda v, a=a, b=b: v.tensor_scalar(
                treeT[:, a:b], treeT[:, a:b], 1e-6, TWO_PI - 1e-6, ALU.max, ALU.min), inc=sV)
            tv = p.counts["sV"]
            p.wait("scalar", sV, tv)
            act(tabS[:, a:b], treeT[:, a:b], AF.Sin, bias=-PI, inc=sS)
            p.wait("vector", sS)
            vop(lambda v, a=a, b=b: v.tensor_scalar(
                xbuf[:, 0:b - a], treeT[:, a:b], 1.5 * PI, -TWO_PI, ALU.is_ge, ALU.mult), inc=sV)
            p.wait("vector", sV)
            vop(lambda v, a=a, b=b: v.tensor_add(
                xbuf[:, 0:b - a], xbuf[:, 0:b - a], treeT[:, a:b]), inc=sV)
            p.wait("vector", sV)
            vop(lambda v, a=a, b=b: v.tensor_scalar(
                xbuf[:, 0:b - a], xbuf[:, 0:b - a], -PI / 2 + 1e-6, 1.5 * PI - 1e-6,
                ALU.max, ALU.min), inc=sV)
            p.wait("scalar", sV)
            act(tabC[:, a:b], xbuf[:, 0:b - a], AF.Sin, bias=-PI / 2, inc=sS)
            for c in chunks:
                tabs_c[c] = p.counts["sS"]

        prev_mask_cons = 0
        oh_spilled = {0: 0, 1: 0}
        prev_passB_P = 0
        prev_head_V = 0
        prev_qev6 = 0
        for h in range(H):
            hb = h % 2
            if h + 1 < H:
                p.wait("gpsimd", sP, prev_passB_P)
                load_head_weights(h + 1)
                g_head_w[h + 1] = p.counts["sGw0" if (h + 1) % 2 == 0 else "sGw1"]
            pre_tab_S = p.counts["sS"]
            pre_tab_V = p.counts["sV"]
            p.wait("vector", sV, wchain_done)
            tables_stage1(h)

            p.wait("gpsimd", sV, prev_mask_cons)
            p.dma("gpsimd", maskh[:], triu_in[0:128, 0:128], sGm)
            mtr = p.counts["sGm"]
            if h < 5:
                csrc = [wqkvuT[:, 3 * D:4 * D], woT_in, wrT_in, wkT_in, wvT_in][h]
                p.dma("gpsimd", wcast[h * D:(h + 1) * D, :], csrc,
                      sGc3 if h < 2 else sGc4)
            p.wait("vector", sGm, mtr)
            p.wait("vector", sV, es1_ready)
            vop(lambda v, hh=h: v.tensor_scalar(
                maskh[:], maskh[:], es1[:, hh:hh + 1], None, ALU.mult), inc=sV)
            mask_ready = p.counts["sV"]

            p.wait("tensor", sGw[h % 2], g_head_w[h])
            if h == 0:
                p.wait("tensor", sGp)
            kb = prm[:, PC_QKVU_B + 8 + h:PC_QKVU_B + 8 + h + 1]
            qb = prm[:, PC_QKVU_B + h:PC_QKVU_B + h + 1]

            startS_A = p.counts["sS"] if h == 0 else pre_tab_S
            startV_A = p.counts["sV"] if h == 0 else pre_tab_V
            kev_d = {-1: p.counts["sS"]}
            klr_d = {-1: prev_head_V, -2: prev_head_V}
            vev_d = {}
            kte_d = {}
            kvo_d = {-1: p.counts["sS"]}
            ka_p = {}
            vp_p = {}
            ktr_p = {}
            kv_p = {-1: 0}

            def ka(c):
                bank = psA if c % 2 == 0 else psC
                p.wait("tensor", sS, kev_d[c - 2] if c >= 2 else startS_A)
                for dc in range(NCH):
                    mm(bank[:], wk_h[:, hb * 1024 + dc * 128: hb * 1024 + (dc + 1) * 128],
                       yT[:, dc * T_SEQ + c * C: dc * T_SEQ + (c + 1) * C],
                       start=(dc == 0), stop=(dc == NCH - 1),
                       inc=(sP if dc == NCH - 1 else None))
                ka_p[c] = p.counts["sP"]

            def kev(c):
                bank = psA if c % 2 == 0 else psC
                p.wait("scalar", sP, ka_p[c])
                p.wait("scalar", sV, klr_d[c - 2])
                act(psil[:, (c % 2) * C:(c % 2 + 1) * C], bank[:], AF.Silu, bias=kb, inc=sS)
                kev_d[c] = p.counts["sS"]

            def klr(c):
                p.wait("vector", sS, kev_d[c])
                p.wait("vector", sS, tabs_c[c])
                ps = psil[:, (c % 2) * C:(c % 2 + 1) * C]
                vop(lambda v, c=c, ps=ps: v.tensor_mul(
                    Kc[:, c * C:(c + 1) * C], ps, tabC[:, c * C:(c + 1) * C]))
                vop(lambda v, c=c, ps=ps: v.tensor_mul(
                    Ks[:, c * C:(c + 1) * C], ps, tabS[:, c * C:(c + 1) * C]), inc=sV)
                klr_d[c] = p.counts["sV"]

            def vproj(c):
                bank = psB if c % 2 == 0 else psO
                p.wait("tensor", sS, vev_d[c - 2] if c >= 2 else startS_A)
                if c < 2:
                    p.wait("tensor", sV, startV_A)
                for j in range(4):
                    for dc in range(NCH):
                        mm(bank[:, j * 128:(j + 1) * 128],
                           yT[:, dc * T_SEQ + c * C + j * 128: dc * T_SEQ + c * C + (j + 1) * 128],
                           wv_h[:, hb * 1024 + dc * 128: hb * 1024 + (dc + 1) * 128],
                           start=(dc == 0), stop=False)
                    mm(bank[:, j * 128:(j + 1) * 128], ones_row,
                       rowsbf[0:1, h * HD: (h + 1) * HD],
                       start=False, stop=True, inc=(sP if j == 3 else None))
                vp_p[c] = p.counts["sP"]

            def vev(c):
                bank = psB if c % 2 == 0 else psO
                p.wait("scalar", sP, vp_p[c])
                act(Vtok[:, c * C:(c + 1) * C], bank[:], AF.Identity, inc=sS)
                vev_d[c] = p.counts["sS"]

            def ktr(c):
                p.wait("tensor", sV, klr_d[c])
                if c == 0:
                    p.wait("tensor", sV, startV_A)
                p.wait("tensor", sS, kte_d[c - 1] if c >= 1 else startS_A)
                for half, srcb in ((0, Kc), (1, Ks)):
                    for j in range(4):
                        tr(psT[:, half * C + j * 128: half * C + (j + 1) * 128],
                           srcb[:, c * C + j * 128: c * C + (j + 1) * 128],
                           inc=(sP if (half, j) == (1, 3) else None))
                ktr_p[c] = p.counts["sP"]

            def kte(c):
                p.wait("scalar", sP, ktr_p[c])
                act(Ktok[:, (c % 2) * 1024:(c % 2 + 1) * 1024], psT[:], AF.Identity, inc=sS)
                kte_d[c] = p.counts["sS"]

            def kv(c):
                p.wait("tensor", sS, vev_d[c])
                p.wait("tensor", sS, kte_d[c])
                p.wait("tensor", sS, kvo_d[c - 1])
                if c == 0:
                    p.wait("tensor", sV, startV_A)
                kb2 = (c % 2) * 1024
                for half in range(2):
                    for j in range(4):
                        mm(psKt[:, half * 128:(half + 1) * 128],
                           Ktok[:, kb2 + half * C + j * 128: kb2 + half * C + (j + 1) * 128],
                           Vtok[:, c * C + j * 128: c * C + (j + 1) * 128],
                           start=(c == 0 and half == 0 and j == 0),
                           stop=(c == NCH - 1 and half == 1 and j == 3))
                for half in range(2):
                    for j in range(4):
                        mm(psKo[:, half * 128:(half + 1) * 128],
                           Ktok[:, kb2 + half * C + j * 128: kb2 + half * C + (j + 1) * 128],
                           Vtok[:, c * C + j * 128: c * C + (j + 1) * 128],
                           start=(half == 0 and j == 0) if True else (j == 0),
                           stop=(half == 1 and j == 3),
                           inc=(sP if (half, j) == (1, 3) else None))
                kv_p[c] = p.counts["sP"]

            def kvo(c):
                p.wait("scalar", sP, kv_p[c])
                act(KVall[:, c * 256:(c + 1) * 256], psKo[:, 0:256], AF.Identity,
                    scale=es1[:, h:h + 1], inc=sS)
                kvo_d[c] = p.counts["sS"]

            ka(0)
            ka(1)
            kev(0)
            klr(0)
            vproj(0)
            tables_stage2(h)
            ktr(0)
            kte(0)
            for c in range(NCH):
                if c + 2 < NCH:
                    ka(c + 2)
                if c + 1 < NCH:
                    kev(c + 1)
                    klr(c + 1)
                    vproj(c + 1)
                vev(c)
                if c + 1 < NCH:
                    ktr(c + 1)
                    kte(c + 1)
                kv(c)
                kvo(c)
                if c == 0:
                    tables_stage3(h)

            p.wait("scalar", sP, kv_p[NCH - 1])
            act(KVtot[:], psKt[:, 0:256], AF.Identity, inc=sS)
            kvtot_done = p.counts["sS"]

            if _dbg("tabs") and h == 0:
                p.wait("sync", sS, tabs_ready)
                p.dma("sync", dbg["d_tabS"], tabS[:], sST)
                p.dma("sync", dbg["d_tabC"], tabC[:], sST)
            if _dbg("kv") and h == 0:
                p.wait("sync", sV)
                p.wait("sync", sS)
                p.dma("sync", dbg["d_Kc"], Kc[:], sST)
                p.dma("sync", dbg["d_Ks"], Ks[:], sST)
                p.dma("sync", dbg["d_Vt"], Vtok[:], sST)

            startS_B = p.counts["sS"]
            startP_B = p.counts["sP"]
            qev_d = {}
            qlr_d = {-1: p.counts["sV"], -2: p.counts["sV"]}
            qev_d[-2] = 0; qev_d[-1] = 0
            qa_p = {}
            ab_p = {}
            am_d = {}
            am_s = {}
            o_p = {}
            oev_d = {-1: p.counts["sS"]}
            abanks = [psB, psC, psO, psKt]

            def qa(c):
                bank = psA if c % 2 == 0 else psMS
                p.wait("tensor", sS, qev_d[c - 2] if c >= 2
                       else (kev_d[6] if c == 0 else startS_A))
                for dc in range(NCH):
                    mm(bank[:], wq_h[:, hb * 1024 + dc * 128: hb * 1024 + (dc + 1) * 128],
                       yT[:, dc * T_SEQ + c * C: dc * T_SEQ + (c + 1) * C],
                       start=(dc == 0), stop=(dc == NCH - 1),
                       inc=(sP if dc == NCH - 1 else None))
                qa_p[c] = p.counts["sP"]

            def qev(c):
                bank = psA if c % 2 == 0 else psMS
                p.wait("scalar", sP, qa_p[c])
                p.wait("scalar", sV, qlr_d[c - 2])
                act(psil[:, (c % 2) * C:(c % 2 + 1) * C], bank[:], AF.Silu, bias=qb, inc=sS)
                qev_d[c] = p.counts["sS"]

            def qlr(c):
                p.wait("vector", sS, qev_d[c])
                p.wait("vector", sP, o_p.get(c - 2, startP_B))
                qs = (c % 2) * 1024
                ps = psil[:, (c % 2) * C:(c % 2 + 1) * C]
                vop(lambda v, c=c, ps=ps, qs=qs: v.tensor_mul(
                    Qt[:, qs:qs + C], ps, tabC[:, c * C:(c + 1) * C]))
                vop(lambda v, c=c, ps=ps, qs=qs: v.tensor_mul(
                    Qt[:, qs + C:qs + 2 * C], ps, tabS[:, c * C:(c + 1) * C]), inc=sV)
                qlr_d[c] = p.counts["sV"]

            def ab(c):
                qs = (c % 2) * 1024
                p.wait("tensor", sV, qlr_d[c])
                for j in range(4):
                    bank = abanks[j]
                    nj = (4 - j) * 128
                    if c == 0:
                        p.wait("tensor", sS,
                               [vev_d[6], kev_d[7], vev_d[7], kvtot_done][j])
                    p.wait("tensor", sV, am_d.get((c - 1, j), 0))
                    p.wait("tensor", sS, am_s.get((c - 1, j), 0))
                    mm(bank[:, 0:nj], Kc[:, c * C + j * 128: c * C + (j + 1) * 128],
                       Qt[:, qs + j * 128: qs + C], start=True, stop=False)
                    mm(bank[:, 0:nj], Ks[:, c * C + j * 128: c * C + (j + 1) * 128],
                       Qt[:, qs + C + j * 128: qs + 2 * C], start=False, stop=True, inc=sP)
                    ab_p[(c, j)] = p.counts["sP"]

            def amev(c):
                ab2 = (c % 2) * 4 * C
                for j in range(4):
                    p.wait("vector", sP, ab_p[(c, j)])
                    if c == 0:
                        p.wait("vector", sV, mask_ready)
                    bank = abanks[j]
                    vop(lambda v, j=j, bk=bank, ab2=ab2: v.tensor_mul(
                        Am[:, ab2 + j * C: ab2 + j * C + 128], bk[:, 0:128],
                        maskh[:]), inc=sV)
                    am_d[(c, j)] = p.counts["sV"]
                    if j < 3:
                        nj = (4 - j) * 128
                        if TRI_SCALAR_EVAC:
                            p.wait("scalar", sP, ab_p[(c, j)])
                            act(Am[:, ab2 + j * C + 128: ab2 + j * C + nj],
                                bank[:, 128:nj], AF.Identity, scale=es1[:, h:h + 1], inc=sS)
                            am_s[(c, j)] = p.counts["sS"]
                        else:
                            vop(lambda v, j=j, bk=bank, ab2=ab2, nj=nj, hh=h: v.tensor_scalar(
                                Am[:, ab2 + j * C + 128: ab2 + j * C + nj],
                                bk[:, 128:nj], es1[:, hh:hh + 1], None, ALU.mult), inc=sV)
                            am_d[(c, j)] = p.counts["sV"]

            def omm(c):
                ab2 = (c % 2) * 4 * C
                qs = (c % 2) * 1024
                p.wait("tensor", sV, am_d[(c, 3)])
                if TRI_SCALAR_EVAC:
                    p.wait("tensor", sS, am_s[(c, 2)])
                else:
                    p.wait("tensor", sV, am_d[(c, 2)])
                p.wait("tensor", sS, oev_d[c - 1])
                p.wait("tensor", sS, kvtot_done)
                p.wait("tensor", sV, sarr_c[c])
                for j in range(4):
                    nj = (4 - j) * 128
                    mm(psKo[:, j * 128:C], Vtok[:, c * C + j * 128: c * C + (j + 1) * 128],
                       Am[:, ab2 + j * C: ab2 + j * C + nj], start=(j == 0), stop=False)
                for half in range(2):
                    mm(psKo[:], KVtot[:, half * 128:(half + 1) * 128],
                       Qt[:, qs + half * C: qs + (half + 1) * C], start=False, stop=False)
                for half in range(2):
                    mm(psKo[:], Sarr[:, c * 256 + half * 128: c * 256 + (half + 1) * 128],
                       Qt[:, qs + half * C: qs + (half + 1) * C], start=False, stop=(half == 1),
                       inc=(sP if half == 1 else None))
                o_p[c] = p.counts["sP"]

            def oev(c):
                p.wait("scalar", sP, o_p[c])
                ohb = c % 2
                p.wait("scalar", sSTo[ohb], oh_spilled[ohb])
                act(Oh[:, ohb * C:(ohb + 1) * C], psKo[:], AF.Identity, inc=sS)
                oev_d[c] = p.counts["sS"]
                p.wait("sync", sS, oev_d[c])
                p.dma("sync", o_spill[h * 128:(h + 1) * 128, c * C:(c + 1) * C],
                      Oh[:, ohb * C:(ohb + 1) * C], sSTo[ohb])
                oh_spilled[ohb] = p.counts["sSTo0" if ohb == 0 else "sSTo1"]

            qa(0)
            qev(0)
            qlr(0)
            p.wait("vector", sS, kvo_d[NCH - 1])
            p.wait("vector", sP, prev_passB_P)
            sarr_c = {}
            vop(lambda v: v.memset(Sfp[:], 0.0), inc=sV)
            p.wait("vector", sV)
            for c in range(NCH):
                vop(lambda v, c=c: v.tensor_copy(Sarr[:, c * 256:(c + 1) * 256], Sfp[:]), inc=sV)
                sarr_c[c] = p.counts["sV"]
                p.wait("vector", sV)
                if c < NCH - 1:
                    def _pfx(v, c=c):
                        with nc.allow_low_precision("fp32 accumulator over bf16 KV'' terms"):
                            return v.tensor_add(Sfp[:], Sfp[:], KVall[:, c * 256:(c + 1) * 256])
                    vop(_pfx, inc=sV)
                    p.wait("vector", sV)

            qa(1)
            ab(0)
            amev(0)
            for c in range(NCH):
                if c + 1 < NCH:
                    qev(c + 1)
                    qlr(c + 1)
                if c + 2 < NCH:
                    qa(c + 2)
                if c + 1 < NCH:
                    ab(c + 1)
                omm(c)
                if c + 1 < NCH:
                    amev(c + 1)
                oev(c)
                if h == H - 1 and c == 0:
                    p.wait("sync", sS, tabs_c[NCH - 1])
                    p.wait("sync", sV, qlr_d[1])
                    p.wait("sync", sGc3)
                    p.dma("sync", WuT[:].rearrange("p (c n) -> p c n", c=NCH),
                          wcast[0:D, :].rearrange("(c p) n -> p c n", p=128), sGw2)
                if h == H - 1 and c == 6:
                    p.wait("sync", sV, qlr_d[7])
                    p.wait("sync", sP, ab_p[(7, 3)])
                    p.dma("sync", WoT[:].rearrange("p (c n) -> p c n", c=NCH),
                          wcast[D:2 * D, :].rearrange("(c p) n -> p c n", p=128), sGw2)
                    p.wait("sync", sSTo[0])
                    for hh in range(H):
                        p.dma("sync", Oslc[:, hh * C:(hh + 1) * C],
                              o_spill[hh * 128:(hh + 1) * 128, 0:C], sOs[0])
                    early_osl0 = (sOs[0], p.counts["sOs0"])
            prev_passB_P = p.counts["sP"]
            prev_mask_cons = p.counts["sV"]
            prev_head_V = p.counts["sV"]
            prev_qev6 = qev_d[6]
            if _dbg("att") and h == 0:
                p.wait("sync", sS, kvtot_done)
                p.wait("sync", sV)
                p.dma("sync", dbg["d_KVt"], KVtot[:], sST)
                p.dma("sync", dbg["d_Sarr"], Sarr[:], sST)
                p.dma("sync", dbg["d_mask"], maskh[:], sST)
            if _dbg("oh") and h == 0:
                p.wait("sync", sSTo[0], oh_spilled[0])
                p.wait("sync", sSTo[1], oh_spilled[1])
                p.dma("sync", dbg["d_Oh"], o_spill[0:128, :], sST)

        p.wait("sync", sSTo[0])
        p.wait("sync", sSTo[1])
        p.wait("sync", sP)
        p.wait("sync", sV)
        p.wait("sync", sS)
        wuwo = p.counts["sGw2"]

        sO = p.sem("sO")
        xrk_done = {}
        prev_rh_P = {-2: 0, -1: 0}
        def pool_op(fn, inc=None):
            return p.emit("gpsimd", fn, inc=inc)

        xrk_dlt = {}

        def xrk_subs(g, ywait):
            sb = (g % 2) * NCH * C
            p.wait("gpsimd", sV, ywait)
            p.wait("gpsimd", sP, prev_rh_P.get(g - 2, 0))
            for dc in range(NCH):
                cur = yT[:, dc * T_SEQ + g * C: dc * T_SEQ + (g + 1) * C]
                if g == 0:
                    prev = yT[:, dc * T_SEQ: dc * T_SEQ + C - 1]
                    pool_op(lambda e, cur=cur, prev=prev, dc=dc: e.tensor_sub(
                        xkb[:, sb + dc * C + 1: sb + (dc + 1) * C], cur[:, 1:C], prev),
                        inc=(sO2 if dc == NCH - 1 else None))
                else:
                    prev = yT[:, dc * T_SEQ + g * C - 1: dc * T_SEQ + (g + 1) * C - 1]
                    pool_op(lambda e, cur=cur, prev=prev, dc=dc: e.tensor_sub(
                        xkb[:, sb + dc * C: sb + (dc + 1) * C], cur, prev),
                        inc=(sO2 if dc == NCH - 1 else None))
            xrk_dlt[g] = p.counts["sO2"]

        def xrk_stt(g, dcs=None):
            sb = (g % 2) * NCH * C
            p.wait("vector", sO2, xrk_dlt[g])
            p.wait("vector", sP, prev_rh_P.get(g - 2, 0))
            for dc in (range(NCH) if dcs is None else dcs):
                mur = prm[:, PC_MU_R + dc:PC_MU_R + dc + 1]
                muk = prm[:, PC_MU_K + dc:PC_MU_K + dc + 1]
                cur = yT[:, dc * T_SEQ + g * C: dc * T_SEQ + (g + 1) * C]
                if g == 0:
                    prev = yT[:, dc * T_SEQ: dc * T_SEQ + C - 1]
                    vop(lambda v, dc=dc, prev=prev, m=mur: v.scalar_tensor_tensor(
                        xrb[:, sb + dc * C + 1: sb + (dc + 1) * C],
                        xkb[:, sb + dc * C + 1: sb + (dc + 1) * C], m, prev, ALU.mult, ALU.add))
                    vop(lambda v, dc=dc, cur=cur, m=mur: v.tensor_scalar(
                        xrb[:, sb + dc * C: sb + dc * C + 1], cur[:, 0:1], m, None, ALU.mult), inc=sV)
                    p.wait("vector", sV)
                    vop(lambda v, dc=dc, prev=prev, m=muk: v.scalar_tensor_tensor(
                        xkb[:, sb + dc * C + 1: sb + (dc + 1) * C],
                        xkb[:, sb + dc * C + 1: sb + (dc + 1) * C], m, prev, ALU.mult, ALU.add))
                    vop(lambda v, dc=dc, cur=cur, m=muk: v.tensor_scalar(
                        xkb[:, sb + dc * C: sb + dc * C + 1], cur[:, 0:1], m, None, ALU.mult), inc=sV)
                    p.wait("vector", sV)
                else:
                    prev = yT[:, dc * T_SEQ + g * C - 1: dc * T_SEQ + (g + 1) * C - 1]
                    vop(lambda v, dc=dc, prev=prev, m=mur: v.scalar_tensor_tensor(
                        xrb[:, sb + dc * C: sb + (dc + 1) * C],
                        xkb[:, sb + dc * C: sb + (dc + 1) * C], m, prev, ALU.mult, ALU.add), inc=sV)
                    p.wait("vector", sV)
                    vop(lambda v, dc=dc, prev=prev, m=muk: v.scalar_tensor_tensor(
                        xkb[:, sb + dc * C: sb + (dc + 1) * C],
                        xkb[:, sb + dc * C: sb + (dc + 1) * C], m, prev, ALU.mult, ALU.add), inc=sV)
                    p.wait("vector", sV)
            xrk_done[g] = p.counts["sV"]

        def xrk_mix(g, ywait):
            xrk_subs(g, ywait)
            xrk_stt(g)


        p.wait("tensor", sGw2, wuwo)
        p.wait("tensor", sS)
        p.wait("tensor", sV)
        pairs = [(psB, psC), (psA, psO)]
        gmult_done = {-1: 0, -2: 0}
        rsb_map = {}
        srow_done = {}
        rsb_ev = {-1: 0}
        osl_map = {}
        zmm_last = {-1: 0, -2: 0}
        zl_done = {-2: 0, -1: 0}
        zmm_zp = {}
        x1st = {0: 0, 1: 0}
        y1tv = {}
        y1recip = {-2: 0, -1: 0}
        ytr_d = {-2: 0, -1: 0}
        yev_d = {-1: 0}
        aggr_z = {}
        u_kr = {}
        u_msp = {}
        u_sq_done = {}

        def oslc_load(g):
            if g == 0:
                osl_map[0] = early_osl0
                return
            p.wait("sync", sO, gmult_done[g - 1])
            p.wait("sync", sP, zmm_last[g - 2])
            base = (g % 2) * H * C
            for hh in range(H):
                p.dma("sync", Oslc[:, base + hh * C: base + (hh + 1) * C],
                      o_spill[hh * 128:(hh + 1) * 128, g * C:(g + 1) * C], sOs[g % 2])
            osl_map[g] = (sOs[g % 2], p.counts[sOs[g % 2]])

        def u_compute(g, hs):
            sl = g % 2
            osl = osl_map[g]
            pend = []

            def flush_row():
                hh0, sqc = pend.pop(0)
                p.wait("tensor", sO, sqc)
                if hh0 == 0:
                    p.wait("tensor", sV, rsb_ev[g - 1])
                mm(psMS[0:1, :], onescol[:, 0:1], o2b[:, (hh0 % 2) * C:(hh0 % 2 + 1) * C],
                   start=(hh0 == 0), stop=(hh0 == H - 1),
                   inc=(sP if hh0 == H - 1 else None))

            for hh in hs:
                bank = psKt if hh % 2 == 0 else psKo
                p.wait("tensor", sS, u_kr.get((g, hh - 2), 0))
                for dc in range(NCH):
                    mm(bank[:], WuT[:, dc * D + hh * 128: dc * D + (hh + 1) * 128],
                       yT[:, dc * T_SEQ + g * C: dc * T_SEQ + (g + 1) * C],
                       start=(dc == 0), stop=(dc == NCH - 1),
                       inc=(sP if dc == NCH - 1 else None))
                up = p.counts["sP"]
                p.wait("gpsimd", *osl)
                p.wait("gpsimd", sP, up)
                sli = Oslc[:, (sl * H + hh) * C:(sl * H + hh + 1) * C]
                pool_op(lambda e, a=sli, hh=hh: e.tensor_mul(
                    o2b[:, (hh % 2) * C:(hh % 2 + 1) * C], a, a), inc=sO)
                pend.append((hh, p.counts["sO"]))
                p.wait("scalar", sP, up)
                if hh == hs[0]:
                    p.wait("scalar", sO, gmult_done[g - 2])
                    p.wait("scalar", *osl)
                ub = prm[:, PC_QKVU_B + 24 + hh:PC_QKVU_B + 24 + hh + 1]
                act(ubuf[:, (sl * H + hh) * C:(sl * H + hh + 1) * C], bank[:],
                    AF.Identity, bias=ub, inc=sS)
                u_kr[(g, hh)] = p.counts["sS"]
                if len(pend) >= 2:
                    flush_row()
            while pend:
                flush_row()
            if hs[-1] == H - 1:
                u_msp[g] = p.counts["sP"]
                u_sq_done[g] = p.counts["sO"]

        def srms_chain(g):
            p.wait("scalar", sP, u_msp[g])
            act(srow[0:1, :], psMS[0:1, :], AF.Sqrt, scale=1.0 / D, bias=EPS_SRMS, inc=sS)
            srow_done[g] = p.counts["sS"]
            p.wait("vector", sS, srow_done[g])
            def _rcp(v):
                with nc.allow_low_precision("rs row cast to bf16 for matmul broadcast"):
                    return v.reciprocal(rsrowb, srow[0:1, :])
            vop(_rcp, inc=sV)
            rsc = p.counts["sV"]
            p.wait("tensor", sV, rsc)
            p.wait("tensor", sS, srow_done[g])
            mm(psMS[:, :], ones_row, rsrowb, start=True, stop=True, inc=sP)
            rsbp = p.counts["sP"]
            p.wait("vector", sP, rsbp)
            p.wait("vector", sO, gmult_done[g - 1])
            vop(lambda v: v.tensor_copy(rsb[:], psMS[:, :]), inc=sV)
            rsb_map[g] = p.counts["sV"]
            rsb_ev[g] = rsb_map[g]

        def gmult(g):
            sl = g % 2
            p.wait("gpsimd", sV, rsb_map[g])
            p.wait("gpsimd", sS, u_kr[(g, H - 1)])
            p.wait("gpsimd", sO, u_sq_done[g])
            p.wait("gpsimd", *osl_map[g])
            for hh in range(H):
                sli = Oslc[:, (sl * H + hh) * C:(sl * H + hh + 1) * C]
                pool_op(lambda e, a=sli: e.tensor_mul(a, a, rsb[:]),
                        inc=(sO if hh == H - 1 else None))
            p.wait("gpsimd", sO)
            for hh in range(H):
                sli = Oslc[:, (sl * H + hh) * C:(sl * H + hh + 1) * C]
                pool_op(lambda e, a=sli, hh=hh: e.tensor_mul(
                    a, a, ubuf[:, (sl * H + hh) * C:(sl * H + hh + 1) * C]),
                    inc=(sO if hh == H - 1 else None))
            gmult_done[g] = p.counts["sO"]

        def zmm(g, tt):
            t = g * 4 + tt
            b = t % 2
            xsl = xbuf[:, b * D:(b + 1) * D]
            p.wait("sync", sV, last_x_cons[b])
            p.dma("sync", xsl, x_in[t * 128:(t + 1) * 128, :], sX[b])
            xl = (sX[b], p.counts[sX[b]])
            pr = pairs[tt % 2]
            base = (g % 2) * H * C
            p.wait("tensor", sO, gmult_done[g])
            p.wait("tensor", sS, zl_done[t - 2])
            for dc in range(NCH):
                for cg in range(2):
                    mm(pr[cg][:], Oslc[:, base + dc * C + tt * 128: base + dc * C + (tt + 1) * 128],
                       WoT[:, dc * D + cg * C: dc * D + (cg + 1) * C],
                       start=(dc == 0), stop=False)
            for cg in range(2):
                mm(pr[cg][:], ones_row, rowsbf[0:1, D + cg * C: D + (cg + 1) * C],
                   start=False, stop=True, inc=(sP if cg == 1 else None))
            zmm_zp[t] = p.counts["sP"]
            zmm_last[g] = p.counts["sP"]
            return xl

        def zchain(g, tt, xl):
            t = g * 4 + tt
            sl2 = tt % 2
            pr = pairs[sl2]
            st = stat6[:, sl2 * 12:(sl2 + 1) * 12]
            mvs = mv[:, sl2 * 2:(sl2 + 1) * 2]
            sq = sqv[:, sl2:sl2 + 1]
            rs = rstd[:, sl2:sl2 + 1]
            bb = b2[:, sl2:sl2 + 1]
            p.wait("vector", sP, zmm_zp[t])
            vop(lambda v, pr=pr, st=st: v.bn_stats(st[:, 0:6], pr[0][:]))
            vop(lambda v, pr=pr, st=st: v.bn_stats(st[:, 6:12], pr[1][:]), inc=sV)
            p.wait("vector", sV)
            vop(lambda v, st=st, mvs=mvs: v.bn_aggr(mvs, st), inc=sV)
            ag = p.counts["sV"]
            p.wait("scalar", sV, ag)
            p.wait("scalar", sV, y1recip[t - 2])
            act(sq, mvs[:, 1:2], AF.Sqrt, bias=EPS_LN, inc=sS)
            sqd = p.counts["sS"]
            p.wait("vector", sS, sqd)
            vop(lambda v, rs=rs, sq=sq: v.reciprocal(rs, sq), inc=sV)
            p.wait("vector", sV)
            vop(lambda v, bb=bb, mvs=mvs, rs=rs: v.tensor_mul(bb, mvs[:, 0:1], rs), inc=sV)
            p.wait("vector", sV)
            vop(lambda v, bb=bb: v.tensor_scalar(bb, bb, -1.0, None, ALU.mult), inc=sV)
            b2d = p.counts["sV"]
            p.wait("scalar", sV, b2d)
            act(ybuf[:, sl2 * D: sl2 * D + C], pr[0][:], AF.Identity, scale=rs, bias=bb)
            act(ybuf[:, sl2 * D + C:(sl2 + 1) * D], pr[1][:], AF.Identity, scale=rs, bias=bb, inc=sS)
            zl_done[t] = p.counts["sS"]
            b = t % 2
            xsl = xbuf[:, b * D:(b + 1) * D]
            p.wait("vector", sS, zl_done[t])
            p.wait("vector", *xl)
            p.wait("vector", sSTx[sl2], x1st[sl2])
            vop(lambda v, a=xsl, sl2=sl2: v.tensor_add(
                x1b[:, sl2 * D:(sl2 + 1) * D], ybuf[:, sl2 * D:(sl2 + 1) * D], a), inc=sV)
            x1v = p.counts["sV"]
            last_x_cons[b] = x1v
            p.wait("sync", sV, x1v)
            p.dma("sync", x1_dram[t * 128:(t + 1) * 128, :], x1b[:, sl2 * D:(sl2 + 1) * D], sSTx[sl2])
            if _dbg("x1"):
                p.dma("sync", dbg["d_x1"][t * 128:(t + 1) * 128, :],
                      x1b[:, sl2 * D:(sl2 + 1) * D], sSTx[sl2])
            x1st[sl2] = p.counts[sSTx[sl2]]

        def y1chain(g, tt):
            t = g * 4 + tt
            sl2 = tt % 2
            st = stat6[:, sl2 * 12:(sl2 + 1) * 12]
            mvs = mv[:, sl2 * 2:(sl2 + 1) * 2]
            sq = sqv[:, sl2:sl2 + 1]
            rs = rstd[:, sl2:sl2 + 1]
            xs = x1b[:, sl2 * D:(sl2 + 1) * D]
            p.wait("vector", sV)
            vop(lambda v, xs=xs, st=st: v.bn_stats(st[:, 0:6], xs[:, 0:512]))
            vop(lambda v, xs=xs, st=st: v.bn_stats(st[:, 6:12], xs[:, 512:1024]), inc=sV)
            p.wait("vector", sV)
            vop(lambda v, st=st, mvs=mvs: v.bn_aggr(mvs, st), inc=sV)
            ag = p.counts["sV"]
            p.wait("scalar", sV, ag)
            act(sq, mvs[:, 1:2], AF.Sqrt, bias=EPS_LN, inc=sS)
            sqd = p.counts["sS"]
            p.wait("vector", sS, sqd)
            vop(lambda v, rs=rs, sq=sq: v.reciprocal(rs, sq), inc=sV)
            y1recip[t] = p.counts["sV"]
            p.wait("vector", sV)
            p.wait("vector", sP, ytr_d[t - 2])
            vop(lambda v, xs=xs, mvs=mvs, rs=rs, sl2=sl2: v.tensor_scalar(
                y1b[:, sl2 * D:(sl2 + 1) * D], xs, mvs[:, 0:1], rs,
                ALU.subtract, ALU.mult), inc=sV)
            y1tv[t] = p.counts["sV"]

        def ytr(g, tt):
            t = g * 4 + tt
            sl2 = tt % 2
            p.wait("tensor", sV, y1tv[t])
            p.wait("tensor", sV, yev_d[t - 1])
            for j in range(NCH):
                tr(psT[:, j * 128:(j + 1) * 128],
                   y1b[:, sl2 * D + j * 128: sl2 * D + (j + 1) * 128],
                   inc=(sP if j == NCH - 1 else None))
            ytr_d[t] = p.counts["sP"]

        def yev(g, tt):
            t = g * 4 + tt
            p.wait("vector", sP, ytr_d[t])
            dst = yT[:].rearrange("p (c n) -> p c n", c=NCH)[:, :, t * 128:(t + 1) * 128]
            vop(lambda v, d=dst: v.tensor_copy(
                d, psT[:].rearrange("p (c n) -> p c n", c=NCH)), inc=sV)
            yev_d[t] = p.counts["sV"]

        oslc_load(0)
        u_compute(0, list(range(H)))
        srms_chain(0)
        gmult(0)
        for g in range(NG):
            if g == NG - 1:
                p.wait("sync", sP, u_msp[NG - 1])
                p.wait("sync", sGc4)
                p.dma("sync", WrT[:].rearrange("p (c n) -> p c n", c=NCH),
                      wcast[2 * D:3 * D, :].rearrange("(c p) n -> p c n", p=128), sGw3)
            if g + 1 < NG:
                oslc_load(g + 1)
            xl0 = zmm(g, 0)
            xl1 = zmm(g, 1)
            zchain(g, 0, xl0)
            if g == 2:
                xrk_subs(0, yev_d[3])
            if g + 1 < NG:
                u_compute(g + 1, list(range(H)))
                srms_chain(g + 1)
                gmult(g + 1)
            if g > 0:
                ytr(g - 1, 3)
                yev(g - 1, 3)
            y1chain(g, 0)
            if g == 7:
                p.wait("vector", sO, gmult_done[6])
                xrk_stt(0, [0, 1])
            xl2 = zmm(g, 2)
            ytr(g, 0)
            yev(g, 0)
            zchain(g, 1, xl1)
            y1chain(g, 1)
            if g == 7:
                xrk_stt(0, [2, 3])
            xl3 = zmm(g, 3)
            ytr(g, 1)
            yev(g, 1)
            zchain(g, 2, xl2)
            y1chain(g, 2)
            if g == 7:
                xrk_stt(0, [4, 5])
            ytr(g, 2)
            yev(g, 2)
            zchain(g, 3, xl3)
            y1chain(g, 3)
            if g == 7:
                xrk_stt(0, [6, 7])
            if g + 1 == NG:
                ytr(g, 3)
                yev(g, 3)

        y1T_done = p.counts["sV"]
        if _dbg("y1T"):
            p.wait("sync", sV, y1T_done)
            p.dma("sync", dbg["d_y1T"], yT[:], sST)

        p.wait("sync", sP)
        p.wait("sync", sV)
        p.wait("sync", sS)
        p.dma("sync", WkT[:].rearrange("p (c n) -> p c n", c=NCH), wcast[3 * D:4 * D, :].rearrange("(c p) n -> p c n", p=128), sGw4)
        p.dma("sync", WvT[:].rearrange("p (c n) -> p c n", c=NCH), wcast[4 * D:5 * D, :].rearrange("(c p) n -> p c n", p=128), sGw5)

        p.wait("tensor", sGw3, 16)
        p.wait("tensor", sS)
        p.wait("tensor", sV)
        f_done = {}
        sig_done = {}
        out_store = {0: 0, 1: 0}
        out_store_st = {0: 0, 1: 0}
        f_done[-1] = 0; f_done[-2] = 0

        for g in range(NG):
            sb = (g % 2) * NCH * C

            def r_mms(tt):
                t = g * 4 + tt
                pr = pairs[t % 2]
                p.wait("tensor", sV, xrk_done[g])
                p.wait("tensor", sV, f_done[t - 2])
                for dc in range(NCH):
                    for cg in range(2):
                        mm(pr[cg][:], xrb[:, sb + dc * C + tt * 128: sb + dc * C + (tt + 1) * 128],
                           WrT[:, dc * D + cg * C: dc * D + (cg + 1) * C],
                           start=(dc == 0), stop=(dc == NCH - 1),
                           inc=(sP if (cg == 1 and dc == NCH - 1) else None))
                return p.counts["sP"]

            def sig(tt, rp):
                t = g * 4 + tt
                pr = pairs[t % 2]
                p.wait("scalar", sP, rp)
                p.wait("scalar", sV, f_done[t - 2])
                act(rbuf[:, (t % 2) * D:(t % 2) * D + C], pr[0][:], AF.Sigmoid)
                act(rbuf[:, (t % 2) * D + C:(t % 2 + 1) * D], pr[1][:], AF.Sigmoid, inc=sS)
                sig_done[t] = p.counts["sS"]

            def h_mms(tt):
                t = g * 4 + tt
                pr = pairs[t % 2]
                if g == 0 and tt == 0:
                    p.wait("tensor", sGw5, 16)
                p.wait("tensor", sS, sig_done[t])
                p.wait("tensor", sO, sq_done)
                for dc in range(NCH):
                    for cg in range(2):
                        mm(pr[cg][:], kact[:, dc * C + tt * 128: dc * C + (tt + 1) * 128],
                           WvT[:, dc * D + cg * C: dc * D + (cg + 1) * C],
                           start=(dc == 0), stop=(dc == NCH - 1),
                           inc=(sP if (cg == 1 and dc == NCH - 1) else None))
                return p.counts["sP"]

            def f_ops(tt, hp):
                t = g * 4 + tt
                sl = t % 2
                pr = pairs[sl]
                fs = fsb[:, sl * D:(sl + 1) * D]
                p.wait("vector", sP, hp)
                p.wait("vector", sS, sig_done[t])
                vop(lambda v, fs=fs, pr=pr, sl=sl: v.tensor_mul(
                    fs[:, 0:C], pr[0][:], rbuf[:, sl * D: sl * D + C]))
                vop(lambda v, fs=fs, pr=pr, sl=sl: v.tensor_mul(
                    fs[:, C:D], pr[1][:], rbuf[:, sl * D + C:(sl + 1) * D]), inc=sV)
                f_done[t] = p.counts["sV"]
                p.wait("vector", sV, f_done[t])
                st = stat6[:, sl * 12:(sl + 1) * 12]
                vop(lambda v, fs=fs, st=st: v.bn_stats(st[:, 0:6], fs[:, 0:512]))
                vop(lambda v, fs=fs, st=st: v.bn_stats(st[:, 6:12], fs[:, 512:1024]), inc=sV)
                p.wait("vector", sV)
                vop(lambda v, st=st, sl=sl: v.bn_aggr(mv[:, sl * 2:(sl + 1) * 2], st), inc=sV)
                return p.counts["sV"]

            def sqrt_op(tt, aggr):
                t = g * 4 + tt
                sl = t % 2
                p.wait("scalar", sV, aggr)
                p.wait("scalar", sV, tail_done[t - 2])
                act(sqv[:, sl:sl + 1], mv[:, sl * 2 + 1:sl * 2 + 2], AF.Sqrt,
                    bias=EPS_LN, inc=sS)
                return p.counts["sS"]

            def tail(tt, sq, x1ct):
                t = g * 4 + tt
                sl = t % 2
                fs = fsb[:, sl * D:(sl + 1) * D]
                p.wait("vector", sS, sq)
                vop(lambda v, sl=sl: v.reciprocal(rstd[:, sl:sl + 1], sqv[:, sl:sl + 1]), inc=sV)
                p.wait("vector", sV)
                vop(lambda v, fs=fs, sl=sl: v.tensor_scalar(
                    ybuf[:, 0:D], fs, mv[:, sl * 2:sl * 2 + 1], rstd[:, sl:sl + 1],
                    ALU.subtract, ALU.mult), inc=sV)
                p.wait("vector", sV)
                p.wait("vector", *x1ct)
                p.wait("vector", sSTu[sl], out_store_st[sl])
                vop(lambda v, sl=sl: v.tensor_add(
                    outb[:, sl * D:(sl + 1) * D], ybuf[:, 0:D], x1l[:, sl * D:(sl + 1) * D]), inc=sV)
                tail_done[t] = p.counts["sV"]
                out_store[sl] = tail_done[t]
                p.wait("sync", sV, tail_done[t])
                p.dma("sync", out_hbm[t * 128:(t + 1) * 128, :], outb[:, sl * D:(sl + 1) * D], sSTu[sl])
                out_store_st[sl] = p.counts[sSTu[sl]]

            if g == 0:
                tail_done = {-2: 0, -1: 0}
            x1c = {}

            def x1_load(tt):
                t = g * 4 + tt
                p.wait("sync", sV, out_store[t % 2])
                if g == 0:
                    p.wait("sync", sSTx[0])
                    p.wait("sync", sSTx[1])
                p.dma("sync", x1l[:, (t % 2) * D:(t % 2 + 1) * D],
                      x1_dram[t * 128:(t + 1) * 128, :], sX1L[t % 2])
                x1c[tt] = (sX1L[t % 2], p.counts[sX1L[t % 2]])

            x1_load(0)
            x1_load(1)
            rp0 = r_mms(0)
            rp1 = r_mms(1)
            sig(0, rp0)
            sig(1, rp1)
            if g + 1 < NG:
                xrk_subs(g + 1, y1T_done)
            kr = {}
            sqk = {-2: 0, -1: 0}
            p.wait("tensor", sV, xrk_done[g])
            if g == 0:
                p.wait("tensor", sGw4, 16)
            for kc in range(NCH):
                bank = psKt if kc % 2 == 0 else psKo
                p.wait("tensor", sS, kr.get(kc - 2, 0))
                for dc in range(NCH):
                    mm(bank[:], WkT[:, dc * D + kc * 128: dc * D + (kc + 1) * 128],
                       xkb[:, sb + dc * C: sb + (dc + 1) * C],
                       start=(dc == 0), stop=(dc == NCH - 1),
                       inc=(sP if dc == NCH - 1 else None))
                kp = p.counts["sP"]
                p.wait("scalar", sP, kp)
                p.wait("scalar", sO, sqk[kc - 2])
                act(o2b[:, (kc % 2) * C:(kc % 2 + 1) * C], bank[:], AF.Relu, inc=sS)
                kr[kc] = p.counts["sS"]
                p.wait("gpsimd", sS, kr[kc])
                pool_op(lambda e, kc=kc: e.tensor_mul(
                    kact[:, kc * C:(kc + 1) * C],
                    o2b[:, (kc % 2) * C:(kc % 2 + 1) * C],
                    o2b[:, (kc % 2) * C:(kc % 2 + 1) * C]), inc=sO)
                sqk[kc] = p.counts["sO"]
            sq_done = p.counts["sO"]

            hp0 = h_mms(0)
            hp1 = h_mms(1)
            a0 = f_ops(0, hp0)
            if g + 1 < NG:
                xrk_stt(g + 1, [0, 1])
            rp2 = r_mms(2)
            s0 = sqrt_op(0, a0)
            sig(2, rp2)
            hp2 = h_mms(2)
            a1 = f_ops(1, hp1)
            if g + 1 < NG:
                xrk_stt(g + 1, [2, 3])
            rp3 = r_mms(3)
            s1 = sqrt_op(1, a1)
            sig(3, rp3)
            tail(0, s0, x1c[0])
            x1_load(2)
            hp3 = h_mms(3)
            a2 = f_ops(2, hp2)
            if g + 1 < NG:
                xrk_stt(g + 1, [4, 5])
            s2 = sqrt_op(2, a2)
            tail(1, s1, x1c[1])
            x1_load(3)
            a3 = f_ops(3, hp3)
            if g + 1 < NG:
                xrk_stt(g + 1, [6, 7])
            s3 = sqrt_op(3, a3)
            tail(2, s2, x1c[2])
            prev_rh_P[g] = p.counts["sP"]
            tail(3, s3, x1c[3])

        p.wait("sync", sSTu[0])
        p.wait("sync", sSTu[1])
        p.wait("sync", sSTx[0])
        p.wait("sync", sSTx[1])
        p.wait("sync", sSTo[0])
        p.wait("sync", sSTo[1])
        if p.counts["sST"] > 0:
            p.wait("sync", sST, p.counts["sST"])

        with nc.Block() as block:
            p.replay(block)

    return nc, sorted(dbg)


_CACHED = {}


def _get_program():
    key = tuple(sorted(DEBUG_TAPS))
    if key not in _CACHED:
        _CACHED[key] = build_program()
    return _CACHED[key]


def make_in_maps(L0_attn_x, L0_attn_slope_rate, L0_attn_qkvu_w, L0_attn_qkvu_b,
                 L0_attn_out_w, L0_attn_out_b, L0_attn_theta,
                 L0_ffn_W_r, L0_ffn_W_k, L0_ffn_W_v, L0_ffn_mu_r, L0_ffn_mu_k):
    f32 = np.float32
    wqkvuT = np.ascontiguousarray(np.asarray(L0_attn_qkvu_w, f32).T)
    woT = np.ascontiguousarray(np.asarray(L0_attn_out_w, f32).T)
    wrT = np.ascontiguousarray(np.asarray(L0_ffn_W_r, f32).T)
    wkT = np.ascontiguousarray(np.asarray(L0_ffn_W_k, f32).T)
    wvT = np.ascontiguousarray(np.asarray(L0_ffn_W_v, f32).T)

    prm = np.zeros((128, PCOLS), f32)
    prm[:, PC_QKVU_B:PC_QKVU_B + 32] = np.asarray(L0_attn_qkvu_b, f32).reshape(32, 128).T
    prm[:, PC_MU_R:PC_MU_R + 8] = np.asarray(L0_ffn_mu_r, f32).reshape(8, 128).T
    prm[:, PC_MU_K:PC_MU_K + 8] = np.asarray(L0_ffn_mu_k, f32).reshape(8, 128).T
    prm[:, PC_THETA:PC_THETA + 8] = np.asarray(L0_attn_theta, f32).reshape(8, 128).T
    prm[:, PC_SLOPE:PC_SLOPE + 8] = np.broadcast_to(
        np.asarray(L0_attn_slope_rate, f32).reshape(1, 8), (128, 8))

    rows = np.zeros((1, ROWS_N), f32)
    rows[0, 0:D] = np.asarray(L0_attn_qkvu_b, f32)[2 * D:3 * D]
    rows[0, D:2 * D] = np.asarray(L0_attn_out_b, f32)
    rows[0, 2 * D:2 * D + 128] = 1.0

    consts = np.broadcast_to(
        np.asarray(CONST_VALS, f32)[None, :], (128, len(CONST_VALS))).copy()
    cones = np.zeros((128, 2), f32)
    cones[:, 0] = 1.0
    ident = np.eye(128, dtype=ml_dtypes.bfloat16)
    triu = np.triu(np.ones((C, C))).astype(ml_dtypes.bfloat16)

    x = np.asarray(L0_attn_x, f32)
    in_maps = []
    for b in range(x.shape[0]):
        in_maps.append(dict(
            x_in=np.ascontiguousarray(x[b]),
            wqkvuT=wqkvuT, woT=woT, wrT=wrT, wkT=wkT, wvT=wvT,
            params=prm, rows=rows, consts=consts, cones=cones,
            ident=ident, triu=triu,
        ))
    return in_maps


def kernel(**inputs):
    nc, _ = _get_program()
    in_maps = make_in_maps(**inputs)
    res = run_bass_kernel_spmd(nc, in_maps, core_ids=list(range(len(in_maps))))
    return np.stack([r["out"] for r in res.results], axis=0)
